# version 05
# baseline (speedup 1.0000x reference)
from contextlib import ExitStack

import numpy as np
import concourse.bass as bass
import concourse.mybir as mybir
from concourse.bass_utils import run_bass_kernel_spmd

F32 = mybir.dt.float32
BF16 = mybir.dt.bfloat16
AF = mybir.ActivationFunctionType
ALU = mybir.AluOpType
AX = mybir.AxisListType

P = 128
CD = 1024
CH = CD // P
NB = 2
NSEQ = 2048
T = NB * NSEQ
NH = 16
HD = 64
SCALE = HD ** -0.5
NBK = 63
DI = 2048
DIC = DI // P
DST = 16
DTR = 64
NE = 8
IF = 2048
EPS = 1e-5
TM = 512
NTM = T // TM

ENGINES = ("sync", "gpsimd", "vector", "scalar", "tensor")

ATTN_LIMIT = {"heads": NH, "score": True, "av": True, "prep": True,
              "aug": True, "vt": True, "dq": True, "dk": True}


class Sem:
    __slots__ = ("h", "n", "name")

    def __init__(self, nc, name):
        self.name = name
        self.h = nc.semaphore(name).__enter__()
        self.n = 0


class Prog:

    def __init__(self, nc):
        self.nc = nc
        self.streams = {e: [] for e in ENGINES}
        self.qeng = {}
        self.relay = {}
        self.relay_cache = {}
        self.strict = {}

    def register_queue(self, sem, eng):
        self.qeng[sem.name] = eng
        self.relay[sem.name] = Sem(self.nc, sem.name + "_r")

    def on(self, eng, fn):
        self.streams[eng].append(fn)

    def wait(self, eng, sem, val=None):
        v = sem.n if val is None else val
        if v <= 0:
            return
        qe = self.qeng.get(sem.name)
        if qe is None or qe == eng:
            self.on(eng, lambda e, s=sem.h, v=v: e.wait_ge(s, v))
            return
        rs = self.relay[sem.name]
        cached = self.relay_cache.get(sem.name)
        if cached is None or cached[0] < v:
            qn = sem.n
            self.on(qe, lambda e, s=sem.h, qn=qn: e.wait_ge(s, qn))
            rs.n += 1
            self.on(qe, lambda e, s=rs.h: e.nop().then_inc(s, 1))
            self.relay_cache[sem.name] = (qn, rs.n)
            cached = (qn, rs.n)
        self.on(eng, lambda e, s=rs.h, v=cached[1]: e.wait_ge(s, v))

    def emit(self, eng, fn, inc=None, inc_n=1):
        strict = self.strict.get(eng)
        if strict is not None:
            if strict.n > 0:
                self.on(eng, lambda e, s=strict.h, v=strict.n: e.wait_ge(s, v))
            if inc is None or inc_n == 0:
                inc, inc_n = strict, 1
        if inc is not None and inc_n > 0:
            inc.n += inc_n

            def go(e, fn=fn, s=inc.h, k=inc_n):
                fn(e).then_inc(s, k)
        else:
            def go(e, fn=fn):
                fn(e)
        self.on(eng, go)

    def dma(self, eng, dst, src, sem=None, **kw):
        if sem is not None:
            sem.n += 16
            if sem.name in self.relay_cache:
                pass

            def go(e, dst=dst, src=src, s=sem.h, kw=kw):
                e.dma_start(out=dst, in_=src, **kw).then_inc(s, 16)
        else:
            def go(e, dst=dst, src=src, kw=kw):
                e.dma_start(out=dst, in_=src, **kw)
        self.on(eng, go)

    def flush(self):
        streams = self.streams
        self.streams = {e: [] for e in ENGINES}
        with self.nc.Block() as block:
            for eng in ENGINES:
                fns = streams[eng]

                def section(e, fns=fns):
                    for f in fns:
                        f(e)
                getattr(block, eng)(section)


class Builder:
    def __init__(self, nc, debug=False):
        self.nc = nc
        self.debug = debug
        self.p = Prog(nc)
        self.ctx = ExitStack()
        self.ld = Sem(nc, "s_ld")
        self.gq = Sem(nc, "s_gq")
        self.acc = Sem(nc, "s_acc")
        self.vec = Sem(nc, "s_vec")
        self.act = Sem(nc, "s_act")
        self.ten = Sem(nc, "s_ten")
        self.bar = Sem(nc, "s_bar")
        self.p.register_queue(self.ld, "sync")
        self.p.register_queue(self.gq, "gpsimd")
        self.p.register_queue(self.acc, "gpsimd")
        self.p.strict["vector"] = self.vec
        self.p.strict["scalar"] = self.act
        self.dram = {}
        self.sb = {}
        self.u = ""
        self._uid = 0

    def _phase_prefix(self):
        self._uid += 1
        self.u = "p%d_" % self._uid

    def declare_dram(self):
        nc = self.nc
        d = self.dram

        def din(name, shape):
            d[name] = nc.dram_tensor(name, list(shape), F32,
                                     kind="ExternalInput")

        din("x_f", (CD, T))
        din("wqkv_t", (CD, 3 * CD))
        din("wproj_t", (CD, CD))
        din("b_proj", (CD,))
        din("rpe_q", (NH, HD, NBK))
        din("rpe_k", (NH, HD, NBK))
        din("rpe_v", (NH, NBK, HD))
        din("rpe_v_t", (NH, HD, NBK))
        din("w_inproj", (CD, 2 * DI))
        din("conv_w", (DI, 4))
        din("conv_b", (DI,))
        din("xproj_w", (DI, DTR + 2 * DST))
        din("dtw", (DTR, DI))
        din("dtb", (DI,))
        din("a_log", (P, DIC * DST))
        din("d_skip", (DI,))
        din("w_outproj", (DI, CD))
        din("ln_w", (CD,))
        din("ln_b", (CD,))
        din("router_w", (CD, NE))
        din("gate_w", (NE, CD, IF))
        din("up_w", (NE, CD, IF))
        din("down_w", (NE, IF, CD))
        din("sh_gate", (CD, IF))
        din("sh_up", (CD, IF))
        din("sh_down", (IF, CD))
        din("ident", (P, P))
        din("mask3", (P, 3 * P))

        d["y_f"] = nc.dram_tensor("y_f", [CD, T], F32, kind="ExternalOutput")
        okind = "ExternalOutput" if self.debug else "Internal"
        d["r1"] = nc.dram_tensor("r1", [CD, T], F32, kind=okind)
        d["r2"] = nc.dram_tensor("r2", [CD, T], F32, kind=okind)
        d["u_buf"] = nc.dram_tensor("u_buf", [CD, T], F32, kind=okind)
        d["moe_acc"] = nc.dram_tensor("moe_acc", [CD, T], F32, kind=okind)
        d["statbuf"] = nc.dram_tensor("statbuf", [2 * T], F32, kind="Internal")
        d["mbuf"] = nc.dram_tensor("mbuf", [NE * T], F32, kind=okind)
        d["nbuf"] = nc.dram_tensor("nbuf", [CD, T], BF16, kind=okind)
        d["bcbuf"] = nc.dram_tensor("bcbuf", [2 * DST * NSEQ], F32,
                                    kind=okind)
        d["pqbuf"] = nc.dram_tensor("pqbuf", [16 * P * 512], BF16,
                                    kind="Internal")
        d["pkbuf"] = nc.dram_tensor("pkbuf", [16 * P * 512], BF16,
                                    kind="Internal")
        d["ebuf"] = nc.dram_tensor("ebuf", [P * 640], BF16, kind="Internal")
        d["rbuf"] = nc.dram_tensor("rbuf", [NSEQ], F32, kind="Internal")
        d["qkvbuf"] = nc.dram_tensor("qkvbuf", [3 * CD * NSEQ], BF16,
                                     kind="Internal")

    def alloc_common(self):
        E = self.ctx.enter_context
        nc, p = self.nc, self.p
        sb = self.sb
        sb["ones_bf"] = E(nc.sbuf_tensor(self.u + "c_ones", [P, 1], BF16))
        sb["ones_f"] = E(nc.sbuf_tensor(self.u + "c_onesf", [P, 1], F32))
        sb["ident_bf"] = E(nc.sbuf_tensor(self.u + "c_idb", [P, P], BF16))
        sb["ident_f"] = E(nc.sbuf_tensor(self.u + "c_idf", [P, P], F32))
        sb["ceps"] = E(nc.sbuf_tensor(self.u + "c_eps", [1, 1], F32))
        sb["c1"] = E(nc.sbuf_tensor(self.u + "c_one", [P, 1], F32))
        p.emit("vector", lambda e: e.memset(sb["ones_bf"][:], 1.0))
        p.emit("vector", lambda e: e.memset(sb["ones_f"][:], 1.0))
        p.emit("vector", lambda e: e.memset(sb["ceps"][:], EPS))
        p.emit("vector", lambda e: e.memset(sb["c1"][:], 1.0), inc=self.vec)
        p.dma("sync", sb["ident_f"][:], self.dram["ident"].ap(), sem=self.ld)
        p.dma("gpsimd", sb["ident_bf"][:], self.dram["ident"].ap(),
              sem=self.gq)

    def barrier(self):
        p, bar = self.p, self.bar
        bar.n += len(ENGINES)
        tgt = bar.n
        for eng in ENGINES:
            p.on(eng, lambda e, s=bar.h: e.nop().then_inc(s, 1))
            p.on(eng, lambda e, s=bar.h, v=tgt: e.wait_ge(s, v))

    def ln_stats(self, srcs, W, sqf, ps_sx, ps_sq, st, mur, rsr):
        p = self.p
        vec, act, ten, ld = self.vec, self.act, self.ten, self.ld
        onesf = self.sb["ones_f"]
        p.wait("scalar", vec, vec.n)
        p.wait("scalar", ld, ld.n)
        for c in range(CH):
            p.emit("scalar", lambda e, c=c: e.activation(
                out=sqf[:, c * W:(c + 1) * W], in_=srcs[c],
                func=AF.Square), inc=act, inc_n=(1 if c == CH - 1 else 0))
        sq_done = act.n
        p.wait("tensor", act, sq_done)
        for c in range(CH):
            p.emit("tensor", lambda e, c=c: e.matmul(
                ps_sx, onesf[:], srcs[c],
                start=(c == 0), stop=(c == CH - 1)))
            p.emit("tensor", lambda e, c=c: e.matmul(
                ps_sq, onesf[:], sqf[:, c * W:(c + 1) * W],
                start=(c == 0), stop=(c == CH - 1)),
                inc=ten, inc_n=(1 if c == CH - 1 else 0))
        mm_done = ten.n
        mu, ex2, tmp, rrow = (st[:, 0:W], st[:, W:2 * W], st[:, 2 * W:3 * W],
                              st[:, 3 * W:4 * W])
        p.wait("scalar", ten, mm_done)
        p.emit("scalar", lambda e: e.mul(out=mu, in_=ps_sx, mul=1.0 / CD))
        p.emit("scalar", lambda e: e.mul(out=ex2, in_=ps_sq, mul=1.0 / CD),
               inc=act)
        p.wait("vector", act, act.n)
        p.emit("vector", lambda e: e.tensor_tensor(out=tmp, in0=mu, in1=mu,
                                                   op=ALU.mult), inc=vec)
        p.wait("vector", vec, vec.n)
        p.emit("vector", lambda e: e.tensor_sub(out=tmp, in0=ex2, in1=tmp),
               inc=vec)
        p.wait("scalar", self.vec, self.vec.n)
        p.emit("scalar", lambda e: e.activation(out=tmp, in_=tmp, func=AF.Sqrt,
                                                bias=self.sb["ceps"][:]),
               inc=act)
        p.wait("vector", act, act.n)
        p.emit("vector", lambda e: e.reciprocal(out=rrow, in_=tmp), inc=vec)
        p.wait("sync", vec, vec.n)
        p.wait("sync", ld, ld.n)
        statbuf = self.dram["statbuf"]
        p.dma("sync", bass.AP(statbuf, 0, [[1, 1], [1, W]]), mu, sem=ld)
        p.dma("sync", bass.AP(statbuf, T, [[1, 1], [1, W]]), rrow, sem=ld)
        p.wait("sync", ld, ld.n)
        p.dma("sync", mur[:, 0:W], bass.AP(statbuf, 0, [[0, P], [1, W]]),
              sem=ld)
        p.dma("sync", rsr[:, 0:W], bass.AP(statbuf, T, [[0, P], [1, W]]),
              sem=ld)
        p.wait("vector", ld, ld.n)

    def phase_ffn(self, src, dst):
        self._phase_prefix()
        nc, p, d = self.nc, self.p, self.dram
        ld, gq, acc, vec, act, ten = (self.ld, self.gq, self.acc, self.vec,
                                      self.act, self.ten)
        p.wait("sync", self.gq, self.gq.n)
        p.wait("sync", self.acc, self.acc.n)
        p.wait("gpsimd", self.ld, self.ld.n)
        with ExitStack() as ctx:
            E = ctx.enter_context
            XT = E(nc.sbuf_tensor(self.u + "f_xt", [P, CH * TM], F32))
            UT = E(nc.sbuf_tensor(self.u + "f_ut", [P, CH * TM], F32))
            XB = E(nc.sbuf_tensor(self.u + "f_xb", [P, CH * TM], BF16))
            TSC = E(nc.sbuf_tensor(self.u + "f_tsc", [P, TM], F32))
            NT = E(nc.sbuf_tensor(self.u + "f_nt", [P, 2 * CH * TM], BF16))
            ST = E(nc.sbuf_tensor(self.u + "f_st", [1, 4 * TM], F32))
            MUR = E(nc.sbuf_tensor(self.u + "f_mur", [P, TM], F32))
            RSR = E(nc.sbuf_tensor(self.u + "f_rsr", [P, TM], F32))
            LNW = E(nc.sbuf_tensor(self.u + "f_lnw", [P, CH], F32))
            LNB = E(nc.sbuf_tensor(self.u + "f_lnb", [P, CH], F32))
            RW = E(nc.sbuf_tensor(self.u + "f_rw", [P, CH * NE], F32))
            MSK = E(nc.sbuf_tensor(self.u + "f_msk", [P, NE], F32))
            MROW = E(nc.sbuf_tensor(self.u + "f_mrow", [P, 1], F32))
            SGT = E(nc.sbuf_tensor(self.u + "f_sgt", [P, 1], F32))
            MT = E(nc.sbuf_tensor(self.u + "f_mt", [NE, P], F32))
            GW = [E(nc.sbuf_tensor(self.u + "f_gw%d" % i, [P, 8 * 1024], BF16))
                  for i in range(2)]
            UW = [E(nc.sbuf_tensor(self.u + "f_uw%d" % i, [P, 8 * 1024], BF16))
                  for i in range(2)]
            DW = [E(nc.sbuf_tensor(self.u + "f_dw%d" % i, [P, 8 * 1024], BF16))
                  for i in range(2)]
            XE = E(nc.sbuf_tensor(self.u + "f_xe", [P, CH * TM], BF16))
            SQ = XE
            HM = E(nc.sbuf_tensor(self.u + "f_hm", [P, 8 * TM], BF16))
            SG = E(nc.sbuf_tensor(self.u + "f_sg", [P, 2 * TM], BF16))
            WREP = E(nc.sbuf_tensor(self.u + "f_wrep", [P, 2 * TM], F32))
            DEV = E(nc.sbuf_tensor(self.u + "f_dev", [P, 4 * TM], F32))

            with ExitStack() as ctx2:
                E2 = ctx2.enter_context
                ps_sx = E2(nc.psum_tensor(self.u + "f_pssx", [1, TM], F32))
                ps_sq = E2(nc.psum_tensor(self.u + "f_pssq", [1, TM], F32))
                ps_lg = E2(nc.psum_tensor(self.u + "f_pslg", [P, NE], F32))
                ps_mt = E2(nc.psum_tensor(self.u + "f_psmt", [NE, P], F32))

                p.dma("sync", LNW[:], bass.AP(d["ln_w"], 0, [[1, P], [P, CH]]),
                      sem=ld, allow_slow_non_contiguous=True)
                p.dma("sync", LNB[:], bass.AP(d["ln_b"], 0, [[1, P], [P, CH]]),
                      sem=ld, allow_slow_non_contiguous=True)
                p.dma("gpsimd", RW[:],
                      bass.AP(d["router_w"], 0, [[NE, P], [P * NE, CH], [1, NE]]),
                      sem=gq)
                consts_ld = ld.n
                consts_gq = gq.n

                router_ten_use = 0
                router_act_use = 0
                for tm in range(NTM):
                    t0 = tm * TM
                    prev_ten = ten.n
                    prev_vec = vec.n
                    prev_act = act.n
                    prev_gq = gq.n
                    p.wait("sync", vec, prev_vec)
                    p.wait("sync", ten, prev_ten)
                    p.wait("sync", act, prev_act)
                    for c in range(CH):
                        p.dma("sync", XT[:, c * TM:(c + 1) * TM],
                              src.ap()[c * P:(c + 1) * P, t0:t0 + TM], sem=ld)
                    p.wait("vector", ld, ld.n)
                    p.wait("vector", ten, prev_ten)
                    srcs = [XT[:, c * TM:(c + 1) * TM] for c in range(CH)]
                    p.wait("scalar", gq, prev_gq)
                    self.ln_stats(srcs, TM, UT, ps_sx[:], ps_sq[:], ST, MUR, RSR)
                    p.wait("vector", ten, ten.n)
                    p.wait("vector", gq, prev_gq)
                    for c in range(CH):
                        p.emit("vector", lambda e, c=c: e.tensor_sub(
                            out=UT[:, c * TM:(c + 1) * TM], in0=srcs[c],
                            in1=MUR[:]), inc=vec)
                        p.wait("vector", vec, vec.n)
                        p.emit("vector", lambda e, c=c: e.tensor_tensor(
                            out=UT[:, c * TM:(c + 1) * TM],
                            in0=UT[:, c * TM:(c + 1) * TM], in1=RSR[:],
                            op=ALU.mult), inc=vec)
                    u_done = vec.n
                    p.wait("gpsimd", vec, u_done)
                    for c in range(CH):
                        p.dma("gpsimd", d["u_buf"].ap()[c * P:(c + 1) * P,
                                                        t0:t0 + TM],
                              UT[:, c * TM:(c + 1) * TM], sem=gq)
                    usrc = [UT[:, c * TM:(c + 1) * TM] for c in range(CH)]
                    self.ln_stats(usrc, TM, XT, ps_sx[:], ps_sq[:], ST, MUR, RSR)
                    p.wait("vector", ld, consts_ld)
                    p.wait("vector", ten, ten.n)
                    for c in range(CH):
                        p.emit("vector", lambda e, c=c: e.tensor_sub(
                            out=XT[:, c * TM:(c + 1) * TM], in0=usrc[c],
                            in1=MUR[:]), inc=vec)
                        p.wait("vector", vec, vec.n)
                        p.emit("vector", lambda e, c=c: e.tensor_tensor(
                            out=XT[:, c * TM:(c + 1) * TM],
                            in0=XT[:, c * TM:(c + 1) * TM], in1=RSR[:],
                            op=ALU.mult), inc=vec)
                        p.wait("vector", vec, vec.n)
                        p.emit("vector", lambda e, c=c: e.tensor_scalar(
                            out=XT[:, c * TM:(c + 1) * TM],
                            in0=XT[:, c * TM:(c + 1) * TM],
                            scalar1=LNW[:, c:c + 1], scalar2=LNB[:, c:c + 1],
                            op0=ALU.mult, op1=ALU.add), inc=vec)
                        p.wait("vector", vec, vec.n)
                        p.emit("vector", lambda e, c=c: e.tensor_copy(
                            out=XB[:, c * TM:(c + 1) * TM],
                            in_=XT[:, c * TM:(c + 1) * TM]), inc=vec)
                    norm_done = vec.n
                    p.wait("gpsimd", vec, norm_done)
                    p.dma("gpsimd",
                          bass.AP(d["nbuf"], t0, [[T, P], [T * P, CH], [1, TM]]),
                          XB[:], sem=gq)
                    p.wait("tensor", vec, norm_done)
                    p.wait("tensor", gq, consts_gq)
                    for tb in range(TM // P):
                        p.wait("tensor", vec, vec.n)
                        p.wait("tensor", act, router_act_use)
                        for c in range(CH):
                            p.emit("tensor", lambda e, c=c, tb=tb: e.matmul(
                                ps_lg[:],
                                XT[:, c * TM + tb * P:c * TM + (tb + 1) * P],
                                RW[:, c * NE:(c + 1) * NE],
                                start=(c == 0), stop=(c == CH - 1)),
                                inc=ten, inc_n=(1 if c == CH - 1 else 0))
                        router_ten_use = ten.n
                        p.wait("vector", ten, ten.n)
                        p.emit("vector", lambda e: e.tensor_reduce(
                            out=MROW[:], in_=ps_lg[:], axis=AX.X, op=ALU.max),
                            inc=vec)
                        p.wait("scalar", vec, vec.n)
                        p.emit("scalar", lambda e: e.activation(
                            out=SGT[:], in_=MROW[:], func=AF.Sigmoid), inc=act)
                        p.wait("vector", act, act.n)
                        p.emit("vector", lambda e: e.tensor_scalar(
                            out=MSK[:], in0=ps_lg[:], scalar1=MROW[:],
                            scalar2=SGT[:], op0=ALU.is_equal, op1=ALU.mult),
                            inc=vec)
                        p.wait("tensor", vec, vec.n)
                        p.emit("tensor", lambda e: e.transpose(
                            ps_mt[:], MSK[:], self.sb["ident_f"][:]), inc=ten)
                        router_ten_use = ten.n
                        p.wait("scalar", ten, ten.n)
                        p.emit("scalar", lambda e: e.copy(out=MT[:],
                                                          in_=ps_mt[:]),
                               inc=act)
                        router_act_use = act.n
                        p.wait("gpsimd", act, act.n)
                        p.dma("gpsimd",
                              bass.AP(d["mbuf"], t0 + tb * P,
                                      [[T, NE], [1, P]]),
                              MT[:], sem=gq)
                    prev_ten = router_ten_use
                mbuf_done = gq.n
                self.barrier()

            ps_g = [E(nc.psum_tensor(self.u + "f_psg%d" % i, [P, TM], F32))
                    for i in range(2)]
            ps_u = [E(nc.psum_tensor(self.u + "f_psu%d" % i, [P, TM], F32))
                    for i in range(2)]
            ps_d = [E(nc.psum_tensor(self.u + "f_psd%d" % i, [P, TM], F32))
                    for i in range(2)]

            acc0 = acc.n
            wready = [0, 0]
            wlast_use = [0, 0]
            silu_done = [0, 0]
            hm_written = [0, 0]
            xe_last_use = 0
            wrep_last_use = [0, 0]
            hm_last_use = 0
            dev_evac = [0] * 8

            def load_weights(it):
                e_, half = divmod(it, 2)
                s = it % 2
                p.wait("gpsimd", ten, wlast_use[s])
                if e_ < NE:
                    g_src = bass.AP(d["gate_w"], e_ * CD * IF + half * 1024,
                                    [[IF, P], [P * IF, 8], [1, 1024]])
                    u_src = bass.AP(d["up_w"], e_ * CD * IF + half * 1024,
                                    [[IF, P], [P * IF, 8], [1, 1024]])
                    dn_src = bass.AP(d["down_w"], e_ * IF * CD + half * 1024 * CD,
                                     [[CD, P], [P * CD, 8], [1, 1024]])
                else:
                    g_src = bass.AP(d["sh_gate"], half * 1024,
                                    [[IF, P], [P * IF, 8], [1, 1024]])
                    u_src = bass.AP(d["sh_up"], half * 1024,
                                    [[IF, P], [P * IF, 8], [1, 1024]])
                    dn_src = bass.AP(d["sh_down"], half * 1024 * CD,
                                     [[CD, P], [P * CD, 8], [1, 1024]])
                p.dma("gpsimd", GW[s][:], g_src, sem=gq)
                p.dma("gpsimd", UW[s][:], u_src, sem=gq)
                p.dma("gpsimd", DW[s][:], dn_src, sem=gq)
                wready[s] = gq.n

            load_weights(0)
            p.wait("sync", gq, mbuf_done)

            psd_read = [0, 0]
            hm_consumed = 0
            xe_last_use = ten.n
            nt_ready = [0, 0]
            nt_use_ten = [0, 0]
            nt_use_vec = [0, 0]
            for it in range(18):
                e_, half = divmod(it, 2)
                s = it % 2
                routed = e_ < NE
                if it + 1 < 18:
                    load_weights(it + 1)
                p.wait("tensor", gq, wready[s])
                for tm in range(NTM):
                    t0 = tm * TM
                    nts = tm % 2
                    p.wait("sync", ten, nt_use_ten[nts])
                    p.wait("sync", vec, nt_use_vec[nts])
                    p.dma("sync", NT[:, nts * CH * TM:(nts + 1) * CH * TM],
                          bass.AP(d["nbuf"], t0, [[T, P], [T * P, CH], [1, TM]]),
                          sem=ld)
                    nt_ready[nts] = ld.n

                    def ntile(c, nts=nts):
                        return NT[:, nts * CH * TM + c * TM:
                                  nts * CH * TM + (c + 1) * TM]

                    if routed:
                        w = tm % 2
                        p.wait("sync", vec, wrep_last_use[w])
                        p.dma("sync", WREP[:, w * TM:(w + 1) * TM],
                              bass.AP(d["mbuf"], e_ * T + t0, [[0, P], [1, TM]]),
                              sem=ld)
                        p.wait("vector", ld, nt_ready[nts])
                        p.wait("vector", ld, ld.n)
                        p.wait("vector", ten, xe_last_use)
                        for c in range(CH):
                            p.emit("vector", lambda e, c=c, w=w, ntile=ntile:
                                   e.tensor_tensor(
                                       out=XE[:, c * TM:(c + 1) * TM],
                                       in0=ntile(c),
                                       in1=WREP[:, w * TM:(w + 1) * TM],
                                       op=ALU.mult),
                                   inc=vec, inc_n=(1 if c == CH - 1 else 0))
                        wrep_last_use[w] = vec.n
                        nt_use_vec[nts] = vec.n
                        p.wait("tensor", vec, vec.n)

                        def rhs(c):
                            return XE[:, c * TM:(c + 1) * TM]
                    else:
                        p.wait("tensor", ld, nt_ready[nts])

                        def rhs(c, ntile=ntile):
                            return ntile(c)

                    p.wait("vector", ten, hm_consumed)
                    for ic in range(8):
                        b = ic % 2
                        p.wait("tensor", act, silu_done[b])
                        p.wait("tensor", vec, hm_written[b])
                        for c in range(CH):
                            p.emit("tensor", lambda e, c=c, ic=ic, s=s, b=b,
                                   rhs=rhs: e.matmul(
                                       ps_g[b][:],
                                       GW[s][:, c * 1024 + ic * P:
                                             c * 1024 + (ic + 1) * P],
                                       rhs(c), start=(c == 0),
                                       stop=(c == CH - 1)))
                        for c in range(CH):
                            p.emit("tensor", lambda e, c=c, ic=ic, s=s, b=b,
                                   rhs=rhs: e.matmul(
                                       ps_u[b][:],
                                       UW[s][:, c * 1024 + ic * P:
                                             c * 1024 + (ic + 1) * P],
                                       rhs(c), start=(c == 0),
                                       stop=(c == CH - 1)),
                                   inc=ten, inc_n=(1 if c == CH - 1 else 0))
                        gu_done = ten.n
                        if routed:
                            xe_last_use = ten.n
                        else:
                            nt_use_ten[nts] = ten.n
                        wlast_use[s] = max(wlast_use[s], ten.n)
                        p.wait("scalar", ten, gu_done)
                        p.emit("scalar", lambda e, b=b: e.activation(
                            out=SG[:, b * TM:(b + 1) * TM], in_=ps_g[b][:],
                            func=AF.Silu), inc=act)
                        silu_done[b] = act.n
                        p.wait("vector", act, silu_done[b])
                        p.emit("vector", lambda e, b=b, ic=ic: e.tensor_tensor(
                            out=HM[:, ic * TM:(ic + 1) * TM], in0=ps_u[b][:],
                            in1=SG[:, b * TM:(b + 1) * TM], op=ALU.mult),
                            inc=vec)
                        hm_written[b] = vec.n
                    hm_all = vec.n
                    p.wait("tensor", vec, hm_all)
                    for oc in range(8):
                        b = oc % 2
                        dv = oc % 4
                        p.wait("scalar", acc, acc.n)
                        p.wait("tensor", act, psd_read[b])
                        for ic in range(8):
                            p.emit("tensor", lambda e, ic=ic, oc=oc, s=s, b=b:
                                   e.matmul(ps_d[b][:],
                                            DW[s][:, ic * 1024 + oc * P:
                                                  ic * 1024 + (oc + 1) * P],
                                            HM[:, ic * TM:(ic + 1) * TM],
                                            start=(ic == 0), stop=(ic == 7)),
                                   inc=ten, inc_n=(1 if ic == 7 else 0))
                        hm_consumed = ten.n
                        wlast_use[s] = max(wlast_use[s], ten.n)
                        p.wait("scalar", ten, ten.n)
                        p.emit("scalar", lambda e, dv=dv, b=b: e.copy(
                            out=DEV[:, dv * TM:(dv + 1) * TM], in_=ps_d[b][:]),
                            inc=act)
                        dev_evac[oc] = act.n
                        psd_read[b] = act.n
                        p.wait("gpsimd", act, dev_evac[oc])
                        kw = {} if it == 0 else {"accum_op": ALU.add}
                        p.dma("gpsimd",
                              d["moe_acc"].ap()[oc * P:(oc + 1) * P,
                                                t0:t0 + TM],
                              DEV[:, dv * TM:(dv + 1) * TM], sem=acc, **kw)
            all_acc = acc.n

            p.wait("sync", acc, acc.n)
            p.wait("sync", gq, gq.n)
            for tm in range(NTM):
                t0 = tm * TM
                for h4 in range(2):
                    cs = [h4 * 4 + i for i in range(4)]
                    prev_gq = gq.n
                    p.wait("sync", vec, vec.n)
                    p.wait("sync", gq, prev_gq)
                    for i, c in enumerate(cs):
                        p.dma("sync", DEV[:, i * TM:(i + 1) * TM],
                              d["moe_acc"].ap()[c * P:(c + 1) * P, t0:t0 + TM],
                              sem=ld)
                        p.dma("sync", UT[:, c * TM:(c + 1) * TM],
                              d["u_buf"].ap()[c * P:(c + 1) * P, t0:t0 + TM],
                              sem=ld)
                        p.dma("sync", XT[:, c * TM:(c + 1) * TM],
                              src.ap()[c * P:(c + 1) * P, t0:t0 + TM], sem=ld)
                    p.wait("vector", ld, ld.n)
                    p.wait("vector", gq, prev_gq)
                    for i, c in enumerate(cs):
                        p.emit("vector", lambda e, i=i, c=c: e.tensor_add(
                            out=DEV[:, i * TM:(i + 1) * TM],
                            in0=DEV[:, i * TM:(i + 1) * TM],
                            in1=UT[:, c * TM:(c + 1) * TM]), inc=vec)
                        p.wait("vector", vec, vec.n)
                        p.emit("vector", lambda e, i=i, c=c: e.tensor_add(
                            out=DEV[:, i * TM:(i + 1) * TM],
                            in0=DEV[:, i * TM:(i + 1) * TM],
                            in1=XT[:, c * TM:(c + 1) * TM]), inc=vec)
                    p.wait("gpsimd", vec, vec.n)
                    for i, c in enumerate(cs):
                        p.dma("gpsimd",
                              dst.ap()[c * P:(c + 1) * P, t0:t0 + TM],
                              DEV[:, i * TM:(i + 1) * TM], sem=gq)
            self.barrier()
            self.p.flush()


    def phase_attn(self, src, dst):
        self._phase_prefix()
        nc, p, d = self.nc, self.p, self.dram
        ld, gq, vec, act, ten = self.ld, self.gq, self.vec, self.act, self.ten
        p.wait("sync", self.gq, self.gq.n)
        p.wait("sync", self.acc, self.acc.n)
        p.wait("gpsimd", self.ld, self.ld.n)
        with ExitStack() as ctx:
            E = ctx.enter_context
            XT = E(nc.sbuf_tensor(self.u + "a_xt", [P, CH * TM], F32))
            UT = E(nc.sbuf_tensor(self.u + "a_ut", [P, CH * TM], F32))
            XLBT = E(nc.sbuf_tensor(self.u + "a_xlbt", [P, CH * TM], BF16))
            ST = E(nc.sbuf_tensor(self.u + "a_st", [1, 4 * TM], F32))
            MUR = E(nc.sbuf_tensor(self.u + "a_mur", [P, TM], F32))
            RSR = E(nc.sbuf_tensor(self.u + "a_rsr", [P, TM], F32))
            QKVE = E(nc.sbuf_tensor(self.u + "a_qkve", [P, 2 * TM], BF16))
            QH = E(nc.sbuf_tensor(self.u + "a_qh", [64, NSEQ], BF16))
            KH = E(nc.sbuf_tensor(self.u + "a_kh", [64, NSEQ], BF16))
            VH = E(nc.sbuf_tensor(self.u + "a_vh", [64, NSEQ], BF16))
            OFU = E(nc.sbuf_tensor(self.u + "a_ofu", [P, CH * NSEQ], BF16))
            QTA = E(nc.sbuf_tensor(self.u + "a_qta", [66, NSEQ], BF16))
            KTA = E(nc.sbuf_tensor(self.u + "a_kta", [66, NSEQ], BF16))
            Q2 = E(nc.sbuf_tensor(self.u + "a_q2", [2, NSEQ], BF16))
            K2 = E(nc.sbuf_tensor(self.u + "a_k2", [2, NSEQ], BF16))
            RPK = E(nc.sbuf_tensor(self.u + "a_rpk", [64, NBK], BF16))
            RPQ = E(nc.sbuf_tensor(self.u + "a_rpq", [64, NBK], BF16))
            RPKD = E(nc.sbuf_tensor(self.u + "a_rpkd", [64, NBK], BF16))
            RPQD = E(nc.sbuf_tensor(self.u + "a_rpqd", [64, NBK], BF16))
            RPK0 = E(nc.sbuf_tensor(self.u + "a_rpk0", [64, 2], BF16))
            RPQ0 = E(nc.sbuf_tensor(self.u + "a_rpq0", [64, 2], BF16))
            RPVT = E(nc.sbuf_tensor(self.u + "a_rpvt", [64, NBK], F32))
            BANDW = E(nc.sbuf_tensor(self.u + "a_bandw", [NBK, 64], BF16))
            FIXB = E(nc.sbuf_tensor(self.u + "a_fixb", [64, 2], BF16))
            FIXR = E(nc.sbuf_tensor(self.u + "a_fixr", [2, 64], BF16))
            VT = E(nc.sbuf_tensor(self.u + "a_vt", [P, 16 * 64], BF16))
            DQTM = [E(nc.sbuf_tensor(self.u + "a_dqtm%d" % i, [P, NBK], BF16))
                    for i in range(2)]
            FQ = [E(nc.sbuf_tensor(self.u + "a_fq%d" % i, [P, 512], BF16))
                  for i in range(2)]
            PQA = E(nc.sbuf_tensor(self.u + "a_pqa", [P, 16 * 384], BF16))
            DKT = E(nc.sbuf_tensor(self.u + "a_dkt", [P, 48 * P], BF16))
            WK = [E(nc.sbuf_tensor(self.u + "a_wk%d" % i, [P, 384], BF16))
                  for i in range(2)]
            EB = E(nc.sbuf_tensor(self.u + "a_eb", [P, 2 * NSEQ], BF16))
            NMX4 = E(nc.sbuf_tensor(self.u + "a_nmx4", [P, 4], F32))
            ETS = E(nc.sbuf_tensor(self.u + "a_ets", [P, 16 * 512], BF16))
            BEXT = E(nc.sbuf_tensor(self.u + "a_bext", [P, 64], BF16))
            BEXTT = E(nc.sbuf_tensor(self.u + "a_bextt", [NBK, 4 * P], BF16))
            MSCR = E(nc.sbuf_tensor(self.u + "a_mscr", [P, 384], BF16))
            MSK3 = E(nc.sbuf_tensor(self.u + "a_msk3", [P, 3 * P], F32))
            ZTILE = E(nc.sbuf_tensor(self.u + "a_zt", [P, P], BF16))
            ONEROW = E(nc.sbuf_tensor(self.u + "a_onerow", [1, 512], BF16))
            AUGK = E(nc.sbuf_tensor(self.u + "a_augk", [1, 512], BF16))
            AUGK2 = E(nc.sbuf_tensor(self.u + "a_augk2", [1, 512], BF16))
            RROWS = E(nc.sbuf_tensor(self.u + "a_rrows", [1, P], F32))
            NMAX = E(nc.sbuf_tensor(self.u + "a_nmax", [P, 1], F32))
            A1 = E(nc.sbuf_tensor(self.u + "a_a1", [P, 1], F32))
            A2 = E(nc.sbuf_tensor(self.u + "a_a2", [P, 1], F32))
            SR = E(nc.sbuf_tensor(self.u + "a_sr", [P, 1], F32))
            RS = E(nc.sbuf_tensor(self.u + "a_rs", [P, 1], F32))
            RB = E(nc.sbuf_tensor(self.u + "a_rb", [P, 1], F32))
            RINV = E(nc.sbuf_tensor(self.u + "a_rinv", [P, 1], F32))
            RREP = E(nc.sbuf_tensor(self.u + "a_rrep", [P, NSEQ], F32))
            XR = E(nc.sbuf_tensor(self.u + "a_xr", [P, 2 * TM], F32))
            OUTT = E(nc.sbuf_tensor(self.u + "a_outt", [P, 2 * TM], F32))
            BPJ = E(nc.sbuf_tensor(self.u + "a_bpj", [P, CH], F32))
            WPB = [E(nc.sbuf_tensor(self.u + "a_wpb%d" % i, [P, CH * P], BF16))
                   for i in range(2)]
            ps_s = E(nc.psum_tensor(self.u + "a_pss", [P, NSEQ], F32))
            ps_o = E(nc.psum_tensor(self.u + "a_pso", [64, 512], F32))
            ps_et = [E(nc.psum_tensor(self.u + "a_pset%d" % i, [P, 1024], BF16))
                     for i in range(2)]
            ps_sm = E(nc.psum_tensor(self.u + "a_psm", [P, 512], F32))

            p.dma("sync", MSK3[:], d["mask3"].ap(), sem=ld)
            p.dma("sync", BPJ[:], bass.AP(d["b_proj"], 0, [[1, P], [P, CH]]),
                  sem=ld, allow_slow_non_contiguous=True)
            p.wait("vector", ld, ld.n)
            p.emit("vector", lambda e: e.memset(ZTILE[:], 0.0))
            p.emit("vector", lambda e: e.memset(ONEROW[:], 1.0))
            p.emit("vector", lambda e: e.memset(K2[0:1, :], 1.0), inc=vec)
            zt_ready = vec.n
            p.wait("gpsimd", vec, zt_ready)
            for zoff in (0, 128, 256, 384, 512):
                p.dma("gpsimd", bass.AP(d["ebuf"], zoff, [[640, P], [1, P]]),
                      ZTILE[:], sem=gq)
            for s4 in range(4):
                p.dma("gpsimd", QTA[65:66, s4 * 512:(s4 + 1) * 512],
                      ONEROW[:], sem=gq)
                p.dma("gpsimd", KTA[64:65, s4 * 512:(s4 + 1) * 512],
                      ONEROW[:], sem=gq)
                p.dma("gpsimd", Q2[1:2, s4 * 512:(s4 + 1) * 512],
                      ONEROW[:], sem=gq)

            wq_ready = [0, 0]
            wq_use = [0, 0]

            def load_wqkv(o, sl):
                p.wait("gpsimd", ten, wq_use[sl])
                p.dma("gpsimd", WPB[sl][:],
                      bass.AP(d["wqkv_t"], o * P,
                              [[3 * CD, P], [P * 3 * CD, CH], [1, P]]),
                      sem=gq)
                wq_ready[sl] = gq.n

            def load_wproj(o, sl):
                p.wait("gpsimd", ten, wq_use[sl])
                p.dma("gpsimd", WPB[sl][:],
                      bass.AP(d["wproj_t"], o * P,
                              [[CD, P], [P * CD, CH], [1, P]]),
                      sem=gq)
                wq_ready[sl] = gq.n

            for b in range(NB):
                tb0 = b * NSEQ
                qkv_st = gq.n
                for tm in range(NSEQ // TM):
                    t0 = tb0 + tm * TM
                    prev_vec, prev_ten, prev_act = vec.n, ten.n, act.n
                    p.wait("sync", vec, prev_vec)
                    p.wait("sync", ten, prev_ten)
                    p.wait("sync", act, prev_act)
                    for c in range(CH):
                        p.dma("sync", XT[:, c * TM:(c + 1) * TM],
                              src.ap()[c * P:(c + 1) * P, t0:t0 + TM], sem=ld)
                    p.wait("vector", ld, ld.n)
                    srcs = [XT[:, c * TM:(c + 1) * TM] for c in range(CH)]
                    self.ln_stats(srcs, TM, UT, ps_o[0:1, 0:TM],
                                  ps_sm[0:1, 0:TM], ST, MUR, RSR)
                    p.wait("vector", ten, ten.n)
                    for c in range(CH):
                        p.emit("vector", lambda e, c=c: e.tensor_sub(
                            out=UT[:, c * TM:(c + 1) * TM], in0=srcs[c],
                            in1=MUR[:]), inc=vec)
                        p.wait("vector", vec, vec.n)
                        p.emit("vector", lambda e, c=c: e.tensor_tensor(
                            out=XLBT[:, c * TM:(c + 1) * TM],
                            in0=UT[:, c * TM:(c + 1) * TM],
                            in1=RSR[:], op=ALU.mult), inc=vec)
                    xlb_done = vec.n
                    p.wait("tensor", vec, xlb_done)
                    ev_last = act.n
                    for o in range(24):
                        sl = o % 2
                        qsl = o % 2
                        if b == 0 and tm == 0 and o == 0:
                            load_wqkv(0, 0)
                        if o + 1 < 24:
                            load_wqkv(o + 1, (o + 1) % 2)
                        elif tm + 1 < NSEQ // TM or b == 0:
                            load_wqkv(0, (o + 1) % 2)
                        p.wait("tensor", gq, wq_ready[sl])
                        p.wait("tensor", act, ev_last)
                        for c in range(CH):
                            p.emit("tensor", lambda e, c=c, sl=sl: e.matmul(
                                ps_sm[:],
                                WPB[sl][:, c * P:(c + 1) * P],
                                XLBT[:, c * TM:(c + 1) * TM],
                                start=(c == 0), stop=(c == CH - 1)),
                                inc=ten, inc_n=(1 if c == CH - 1 else 0))
                        wq_use[sl] = ten.n
                        p.wait("scalar", ten, ten.n)
                        p.wait("scalar", gq, qkve_st if (b + tm + o) else 0)
                        if o < 8:
                            p.emit("scalar", lambda e, qsl=qsl: e.activation(
                                out=QKVE[:, qsl * TM:(qsl + 1) * TM],
                                in_=ps_sm[:], func=AF.Identity, scale=SCALE),
                                inc=act)
                        else:
                            p.emit("scalar", lambda e, qsl=qsl: e.copy(
                                out=QKVE[:, qsl * TM:(qsl + 1) * TM],
                                in_=ps_sm[:]), inc=act)
                        ev_last = act.n
                        p.wait("gpsimd", act, ev_last)
                        p.dma("gpsimd",
                              bass.AP(d["qkvbuf"], o * P * NSEQ + tm * TM,
                                      [[NSEQ, P], [1, TM]]),
                              QKVE[:, qsl * TM:(qsl + 1) * TM], sem=gq)
                        qkve_st = gq.n
                qkv_written = gq.n

                of_scaled = vec.n
                augk_dma = 0
                for h in range(ATTN_LIMIT["heads"]):
                    hc, hb = h // 2, (h % 2) * 64
                    p.wait("sync", gq, qkv_written)
                    p.wait("sync", ten, ten.n)
                    p.wait("sync", vec, vec.n)
                    p.wait("sync", act, act.n)
                    p.dma("sync", QH[:],
                          bass.AP(d["qkvbuf"], h * 64 * NSEQ,
                                  [[NSEQ, 64], [1, NSEQ]]), sem=ld)
                    p.dma("sync", KH[:],
                          bass.AP(d["qkvbuf"], (CD + h * 64) * NSEQ,
                                  [[NSEQ, 64], [1, NSEQ]]), sem=ld)
                    p.dma("sync", VH[:],
                          bass.AP(d["qkvbuf"], (2 * CD + h * 64) * NSEQ,
                                  [[NSEQ, 64], [1, NSEQ]]), sem=ld)
                    qkvh_ld = ld.n
                    p.wait("gpsimd", vec, vec.n)
                    p.wait("gpsimd", ten, ten.n)
                    p.dma("gpsimd", RPK[:],
                          bass.AP(d["rpe_k"], h * HD * NBK,
                                  [[NBK, 64], [1, NBK]]), sem=gq)
                    p.dma("gpsimd", RPQ[:],
                          bass.AP(d["rpe_q"], h * HD * NBK,
                                  [[NBK, 64], [1, NBK]]), sem=gq)
                    p.dma("gpsimd", BANDW[:],
                          bass.AP(d["rpe_v"], h * NBK * HD,
                                  [[HD, NBK], [1, HD]]), sem=gq)
                    p.dma("gpsimd", RPVT[:],
                          bass.AP(d["rpe_v_t"], h * HD * NBK,
                                  [[NBK, 64], [1, NBK]]), sem=gq)
                    rpe_ld = gq.n
                    p.wait("vector", gq, rpe_ld)
                    p.emit("vector", lambda e: e.tensor_sub(
                        out=RPKD[:], in0=RPK[:],
                        in1=RPK[:, 0:1].broadcast_to((64, NBK))))
                    p.emit("vector", lambda e: e.tensor_sub(
                        out=RPQD[:], in0=RPQ[:],
                        in1=RPQ[:, 62:63].broadcast_to((64, NBK))))
                    p.emit("vector", lambda e: e.tensor_copy(
                        out=RPK0[:, 0:1], in_=RPK[:, 0:1]))
                    p.emit("vector", lambda e: e.tensor_sub(
                        out=RPK0[:, 1:2], in0=RPK[:, 62:63], in1=RPK[:, 0:1]))
                    p.emit("vector", lambda e: e.tensor_copy(
                        out=RPQ0[:, 0:1], in_=RPQ[:, 62:63]))
                    p.emit("vector", lambda e: e.tensor_sub(
                        out=RPQ0[:, 1:2], in0=RPQ[:, 0:1], in1=RPQ[:, 62:63]))
                    p.emit("vector", lambda e: e.tensor_scalar_mul(
                        out=FIXB[:, 0:1], in0=RPVT[:, 0:1], scalar1=-1.0))
                    p.emit("vector", lambda e: e.tensor_sub(
                        out=FIXB[:, 1:2], in0=RPVT[:, 62:63],
                        in1=RPVT[:, 0:1]), inc=vec)
                    rpe_cols = vec.n

                    p.wait("gpsimd", ld, qkvh_ld)
                    p.dma("gpsimd", QTA[0:64, :], QH[:], sem=gq)
                    p.dma("gpsimd", KTA[0:64, :], KH[:], sem=gq)
                    qta_ld = gq.n

                    if not ATTN_LIMIT["prep"]:
                        of_scaled = vec.n
                        continue
                    p.wait("tensor", gq, qta_ld)
                    p.wait("tensor", vec, rpe_cols)
                    p.wait("tensor", act, act.n)
                    p.wait("tensor", vec, of_scaled)
                    for tm in range(4 if ATTN_LIMIT["aug"] else 0):
                        s0, s1 = tm * 512, (tm + 1) * 512
                        p.wait("tensor", act, act.n)
                        p.emit("tensor", lambda e, s0=s0, s1=s1: e.matmul(
                            ps_s[0:1, 0:512], RPK0[:, 0:1],
                            QTA[0:64, s0:s1], start=True, stop=True))
                        p.emit("tensor", lambda e, s0=s0, s1=s1: e.matmul(
                            ps_s[0:1, 512:1024], RPK0[:, 1:2],
                            QTA[0:64, s0:s1], start=True, stop=True))
                        p.emit("tensor", lambda e, s0=s0, s1=s1: e.matmul(
                            ps_s[0:1, 1024:1536], RPQ0[:, 0:1],
                            KTA[0:64, s0:s1], start=True, stop=True))
                        p.emit("tensor", lambda e, s0=s0, s1=s1: e.matmul(
                            ps_s[0:1, 1536:2048], RPQ0[:, 1:2],
                            KTA[0:64, s0:s1], start=True, stop=True),
                            inc=ten)
                        p.wait("scalar", ten, ten.n)
                        p.wait("scalar", gq, augk_dma)
                        p.emit("scalar", lambda e, s0=s0, s1=s1: e.copy(
                            out=QTA[64:65, s0:s1], in_=ps_s[0:1, 0:512]))
                        p.emit("scalar", lambda e, s0=s0, s1=s1: e.copy(
                            out=Q2[0:1, s0:s1], in_=ps_s[0:1, 512:1024]))
                        p.emit("scalar", lambda e: e.activation(
                            out=AUGK[:], in_=ps_s[0:1, 1024:1536],
                            func=AF.Identity, scale=SCALE))
                        p.emit("scalar", lambda e: e.activation(
                            out=AUGK2[:], in_=ps_s[0:1, 1536:2048],
                            func=AF.Identity, scale=SCALE), inc=act)
                        p.wait("gpsimd", act, act.n)
                        p.dma("gpsimd", KTA[65:66, s0:s1], AUGK[:], sem=gq)
                        p.dma("gpsimd", K2[1:2, s0:s1], AUGK2[:], sem=gq)
                        augk_dma = gq.n
                    aug_dma = gq.n

                    p.emit("tensor", lambda e: e.matmul(
                        ps_sm[0:2, 0:64], FIXB[:],
                        self.sb["ident_bf"][0:64, 0:64], start=True,
                        stop=True), inc=ten)
                    p.wait("scalar", ten, ten.n)
                    p.emit("scalar", lambda e: e.copy(out=FIXR[:],
                                                      in_=ps_sm[0:2, 0:64]),
                           inc=act)
                    p.wait("gpsimd", act, act.n)
                    p.wait("gpsimd", gq, rpe_ld)
                    p.dma("gpsimd", BANDW[0:1, :], FIXR[0:1, :], sem=gq)
                    p.dma("gpsimd", BANDW[62:63, :], FIXR[1:2, :], sem=gq)
                    bandw_ready = gq.n

                    p.wait("vector", ld, qkvh_ld)
                    p.wait("vector", gq, rpe_ld)
                    p.emit("vector", lambda e: e.tensor_scalar_add(
                        out=VH[:], in0=VH[:], scalar1=RPVT[:, 0:1]), inc=vec)
                    vbar_done = vec.n
                    p.wait("tensor", vec, vbar_done)
                    vt_ev = {0: act.n, 1: act.n}
                    for jt in range(16 if ATTN_LIMIT["vt"] else 0):
                        sl = jt % 2
                        p.wait("tensor", act, vt_ev[sl])
                        p.emit("tensor", lambda e, jt=jt, sl=sl: e.transpose(
                            ps_et[sl][:, 0:64], VH[:, jt * P:(jt + 1) * P],
                            self.sb["ident_bf"][0:64, 0:64]), inc=ten)
                        p.wait("scalar", ten, ten.n)
                        p.emit("scalar", lambda e, jt=jt, sl=sl: e.copy(
                            out=VT[:, jt * 64:(jt + 1) * 64],
                            in_=ps_et[sl][:, 0:64]), inc=act)
                        vt_ev[sl] = act.n
                    vt_done = act.n

                    pw_mm = {}
                    pw_ev = {0: act.n, 1: act.n}
                    pw_vec = {0: 0, 1: 0}
                    pw_dma = {0: 0, 1: 0}
                    for qb in range(16 if ATTN_LIMIT["dq"] else 0):
                        bk = qb % 4
                        sl = qb % 2
                        p.wait("tensor", act, pw_mm.get(bk, 0))
                        p.emit("tensor", lambda e, qb=qb, bk=bk: e.matmul(
                            ps_s[:, bk * 512:bk * 512 + NBK],
                            QTA[0:64, qb * P:(qb + 1) * P],
                            RPKD[:], start=True, stop=True), inc=ten)
                        p.wait("scalar", ten, ten.n)
                        p.wait("scalar", vec, pw_vec[sl])
                        p.emit("scalar", lambda e, sl=sl, bk=bk: e.copy(
                            out=DQTM[sl][:],
                            in_=ps_s[:, bk * 512:bk * 512 + NBK]), inc=act)
                        pw_mm[bk] = act.n
                        pw_ev[sl] = act.n
                        p.wait("vector", act, pw_ev[sl])
                        p.wait("vector", gq, pw_dma[sl])
                        p.emit("vector", lambda e, sl=sl: e.memset(
                            FQ[sl][:, 0:225], 0.0))
                        p.emit("vector", lambda e, sl=sl: e.tensor_copy(
                            out=FQ[sl][:, 225:286], in_=DQTM[sl][:, 1:62]))
                        p.emit("vector", lambda e, sl=sl: e.tensor_copy(
                            out=FQ[sl][:, 286:512],
                            in_=DQTM[sl][:, 62:63].broadcast_to((P, 226))),
                            inc=vec)
                        pw_vec[sl] = vec.n
                        p.wait("gpsimd", vec, pw_vec[sl])
                        p.dma("gpsimd",
                              bass.AP(d["pqbuf"], qb * P * 512,
                                      [[512, P], [1, 512]]), FQ[sl][:], sem=gq)
                        pw_dma[sl] = gq.n
                    pq_written = gq.n
                    p.wait("gpsimd", gq, pq_written)
                    p.wait("gpsimd", ten, ten.n)
                    for qb in range(16 if ATTN_LIMIT["dq"] else 0):
                        p.dma("gpsimd", PQA[:, qb * 384:(qb + 1) * 384],
                              bass.AP(d["pqbuf"], qb * P * 512 + 127,
                                      [[511, P], [1, 384]]), sem=gq)
                    pqa_ready = gq.n

                    pw_mm = {}
                    pw_ev = {0: act.n, 1: act.n}
                    pw_vec = {0: 0, 1: 0}
                    pw_dma = {0: 0, 1: 0}
                    for jb in range(16 if ATTN_LIMIT["dk"] else 0):
                        bk = jb % 4
                        sl = jb % 2
                        p.wait("tensor", act, pw_mm.get(bk, 0))
                        p.emit("tensor", lambda e, jb=jb, bk=bk: e.matmul(
                            ps_s[:, bk * 512:bk * 512 + NBK],
                            KTA[0:64, jb * P:(jb + 1) * P],
                            RPQD[:], start=True, stop=True), inc=ten)
                        p.wait("scalar", ten, ten.n)
                        p.wait("scalar", vec, pw_vec[sl])
                        p.emit("scalar", lambda e, sl=sl, bk=bk: e.activation(
                            out=DQTM[sl][:],
                            in_=ps_s[:, bk * 512:bk * 512 + NBK],
                            func=AF.Identity, scale=SCALE), inc=act)
                        pw_mm[bk] = act.n
                        pw_ev[sl] = act.n
                        p.wait("vector", act, pw_ev[sl])
                        p.wait("vector", gq, pw_dma[sl])
                        p.emit("vector", lambda e, sl=sl: e.tensor_copy(
                            out=FQ[sl][:, 0:225],
                            in_=DQTM[sl][:, 0:1].broadcast_to((P, 225))))
                        p.emit("vector", lambda e, sl=sl: e.tensor_copy(
                            out=FQ[sl][:, 225:286], in_=DQTM[sl][:, 1:62]))
                        p.emit("vector", lambda e, sl=sl: e.memset(
                            FQ[sl][:, 286:512], 0.0), inc=vec)
                        pw_vec[sl] = vec.n
                        p.wait("gpsimd", vec, pw_vec[sl])
                        p.dma("gpsimd",
                              bass.AP(d["pkbuf"], jb * P * 512,
                                      [[512, P], [1, 512]]), FQ[sl][:], sem=gq)
                        pw_dma[sl] = gq.n
                    pk_written = gq.n
                    dkt_ev = {0: act.n, 1: act.n}
                    wk_use = {0: 0, 1: 0}
                    p.wait("sync", gq, pk_written)
                    for jb in range(16 if ATTN_LIMIT["dk"] else 0):
                        wsl = jb % 2
                        p.wait("sync", ten, wk_use[wsl])
                        p.dma("sync", WK[wsl][:],
                              bass.AP(d["pkbuf"], jb * P * 512 + 127,
                                      [[511, P], [1, 384]]), sem=ld)
                        wk_ld = ld.n
                        p.wait("tensor", ld, wk_ld)
                        for s3 in range(3):
                            qb = jb - 1 + s3
                            if qb < 0 or qb > 15:
                                continue
                            sl = s3 % 2
                            pair = qb * 3 + (jb - qb + 1)
                            p.wait("tensor", act, dkt_ev[sl])
                            p.emit("tensor", lambda e, s3=s3, sl=sl, wsl=wsl:
                                   e.transpose(
                                       ps_et[sl][:, 0:P],
                                       WK[wsl][:, s3 * P:(s3 + 1) * P],
                                       self.sb["ident_bf"][:]), inc=ten)
                            wk_use[wsl] = ten.n
                            p.wait("scalar", ten, ten.n)
                            p.emit("scalar", lambda e, pair=pair, sl=sl:
                                   e.copy(out=DKT[:, pair * P:(pair + 1) * P],
                                          in_=ps_et[sl][:, 0:P]), inc=act)
                            dkt_ev[sl] = act.n
                    dkt_done = act.n

                    if not ATTN_LIMIT["score"]:
                        of_scaled = vec.n
                        continue
                    p.wait("tensor", gq, aug_dma)
                    p.wait("tensor", act, dkt_done)
                    p.wait("tensor", gq, pqa_ready)
                    p.wait("tensor", gq, bandw_ready)
                    p.wait("tensor", act, vt_done)
                    ebs_guard = [
                        {"ten": 0, "vec": 0, "gq": 0, "act": 0},
                        {"ten": 0, "vec": 0, "gq": 0, "act": 0}]
                    bx_used = 0
                    eb_band_ld = 0
                    pend_et = None
                    for qm in range(4):
                        for qbl in range(4):
                            qb = qm * 4 + qbl
                            q0 = qb * P
                            ebsl = qb % 2
                            eb0 = ebsl * NSEQ
                            p.wait("tensor", act, act.n)
                            p.wait("tensor", vec, vec.n)
                            for jm in range(4):
                                j0m = jm * 512
                                r2lo = max(q0 + 256, j0m)
                                r2hi = min(2048, j0m + 512)
                                plo = max(q0 - 128, 0, j0m)
                                phi = min(q0 + 256, 2048, j0m + 512)
                                dkops = [jb2 for jb2 in (qb - 1, qb, qb + 1)
                                         if 0 <= jb2 <= 15
                                         and j0m <= jb2 * P < j0m + 512]
                                nops = (1 + (1 if r2hi > r2lo else 0)
                                        + (1 if phi > plo else 0) + len(dkops))
                                k = 1
                                p.emit("tensor", lambda e, q0=q0, jm=jm,
                                       last=(k == nops): e.matmul(
                                           ps_s[:, jm * 512:(jm + 1) * 512],
                                           QTA[:, q0:q0 + P],
                                           KTA[:, jm * 512:(jm + 1) * 512],
                                           start=True, stop=last),
                                       inc=ten, inc_n=(1 if k == nops else 0))
                                if r2hi > r2lo:
                                    k += 1
                                    p.emit("tensor", lambda e, q0=q0,
                                           r2lo=r2lo, r2hi=r2hi,
                                           last=(k == nops): e.matmul(
                                               ps_s[:, r2lo:r2hi],
                                               Q2[:, q0:q0 + P],
                                               K2[:, r2lo:r2hi],
                                               start=False, stop=last),
                                           inc=ten,
                                           inc_n=(1 if k == nops else 0))
                                if phi > plo:
                                    k += 1
                                    off = plo - (q0 - 128)
                                    p.emit("tensor", lambda e, qb=qb, plo=plo,
                                           phi=phi, off=off,
                                           last=(k == nops): e.matmul(
                                               ps_s[:, plo:phi],
                                               self.sb["ident_bf"][:],
                                               PQA[:, qb * 384 + off:
                                                   qb * 384 + off + phi - plo],
                                               start=False, stop=last),
                                           inc=ten,
                                           inc_n=(1 if k == nops else 0))
                                for jb2 in dkops:
                                    k += 1
                                    pair = qb * 3 + (jb2 - qb + 1)
                                    p.emit("tensor", lambda e, jb2=jb2,
                                           pair=pair, last=(k == nops):
                                           e.matmul(
                                               ps_s[:, jb2 * P:(jb2 + 1) * P],
                                               self.sb["ident_bf"][:],
                                               DKT[:, pair * P:(pair + 1) * P],
                                               start=False, stop=last),
                                           inc=ten,
                                           inc_n=(1 if k == nops else 0))
                                bank_done = ten.n
                                p.wait("vector", ten, bank_done)
                                p.emit("vector", lambda e, jm=jm: e.tensor_reduce(
                                    out=NMX4[:, jm:jm + 1],
                                    in_=ps_s[:, jm * 512:(jm + 1) * 512],
                                    axis=AX.X, op=ALU.max),
                                    inc=vec, inc_n=(1 if jm == 3 else 0))
                            qk_done = ten.n
                            if pend_et is not None:
                                pend_et()
                                pend_et = None
                            p.wait("vector", vec, vec.n)
                            p.emit("vector", lambda e: e.tensor_reduce(
                                out=NMAX[:], in_=NMX4[:], axis=AX.X,
                                op=ALU.max, negate=True), inc=vec)
                            p.wait("vector", vec, vec.n)
                            nmax_done = vec.n
                            jsplit = min(q0 + 256, 2048)
                            p.wait("scalar", vec, nmax_done)
                            g_ = ebs_guard[ebsl]
                            p.wait("scalar", vec, g_["vec"])
                            p.wait("scalar", ten, g_["ten"])
                            p.wait("scalar", gq, g_["gq"])
                            p.wait("scalar", act, g_["act"])
                            p.emit("scalar", lambda e, jsplit=jsplit, eb0=eb0:
                                   e.activation(
                                       out=EB[:, eb0:eb0 + jsplit],
                                       in_=ps_s[:, 0:jsplit], func=AF.Exp,
                                       bias=NMAX[:], accum_out=A1[:]),
                                   inc=act)
                            if jsplit < 2048:
                                p.emit("scalar", lambda e, jsplit=jsplit,
                                       eb0=eb0: e.activation(
                                           out=EB[:, eb0 + jsplit:eb0 + 2048],
                                           in_=ps_s[:, jsplit:2048],
                                           func=AF.Exp, bias=NMAX[:],
                                           accum_out=A2[:]), inc=act)
                            else:
                                p.wait("vector", act, act.n)
                                p.emit("vector", lambda e: e.memset(
                                    A2[:], 0.0), inc=vec)
                            exp_done = act.n
                            p.wait("vector", act, exp_done)
                            slo = max(q0 - 128, 0)
                            shi = min(q0 + 256, 2048)
                            moff = slo - (q0 - 128)
                            p.emit("vector", lambda e, slo=slo, shi=shi,
                                   moff=moff, eb0=eb0:
                                   e.scalar_tensor_tensor(
                                       out=MSCR[:, 0:shi - slo],
                                       in0=EB[:, eb0 + slo:eb0 + shi],
                                       scalar=1.0,
                                       in1=MSK3[:, moff:moff + shi - slo],
                                       op0=ALU.mult, op1=ALU.mult,
                                       accum_out=SR[:]), inc=vec)
                            p.wait("vector", vec, vec.n)
                            p.emit("vector", lambda e: e.tensor_add(
                                out=RS[:], in0=A1[:], in1=A2[:]), inc=vec)
                            p.wait("vector", vec, vec.n)
                            p.emit("vector", lambda e: e.tensor_add(
                                out=A2[:], in0=A2[:], in1=SR[:]))
                            p.emit("vector", lambda e: e.reciprocal(
                                out=RINV[:], in_=RS[:]), inc=vec)
                            stats_done = vec.n
                            p.wait("tensor", vec, stats_done)
                            p.wait("tensor", act, act.n)
                            p.emit("tensor", lambda e: e.matmul(
                                ps_sm[0:1, 0:P], RINV[:],
                                self.sb["ident_f"][:], start=True, stop=True),
                                inc=ten)
                            p.wait("scalar", ten, ten.n)
                            p.wait("scalar", ld, rrow_ld if (qb or h or b) else 0)
                            p.emit("scalar", lambda e: e.copy(
                                out=RROWS[:], in_=ps_sm[0:1, 0:P]), inc=act)
                            p.wait("sync", act, act.n)
                            p.dma("sync",
                                  bass.AP(d["rbuf"], q0, [[1, 1], [1, P]]),
                                  RROWS[:], sem=ld)
                            rrow_ld = ld.n
                            p.wait("gpsimd", act, exp_done)
                            p.wait("gpsimd", gq, eb_band_ld)
                            if qb == 0:
                                p.dma("gpsimd",
                                      bass.AP(d["ebuf"], 128,
                                              [[640, P], [1, P]]),
                                      ZTILE[:], sem=gq)
                                p.wait("gpsimd", gq, gq.n)
                            if qb == 15:
                                p.dma("gpsimd",
                                      bass.AP(d["ebuf"], 384,
                                              [[640, P], [1, P]]),
                                      ZTILE[:], sem=gq)
                                p.wait("gpsimd", gq, gq.n)
                            p.dma("gpsimd",
                                  bass.AP(d["ebuf"], 128 + moff,
                                          [[640, P], [1, shi - slo]]),
                                  EB[:, eb0 + slo:eb0 + shi], sem=gq)
                            p.wait("gpsimd", gq, gq.n)
                            p.wait("gpsimd", ten, bx_used)
                            p.wait("gpsimd", vec, vec.n)
                            p.dma("gpsimd", BEXT[:, 1:62],
                                  bass.AP(d["ebuf"], 226,
                                          [[641, P], [1, 61]]),
                                  sem=gq)
                            eb_band_ld = gq.n
                            p.wait("vector", gq, eb_band_ld)
                            p.wait("vector", act, act.n)
                            p.emit("vector", lambda e: e.tensor_reduce(
                                out=RB[:], in_=BEXT[:, 1:62],
                                axis=AX.X, op=ALU.add), inc=vec)
                            p.wait("vector", vec, vec.n)
                            p.emit("vector", lambda e: e.tensor_copy(
                                out=BEXT[:, 0:1], in_=RB[:]))
                            p.emit("vector", lambda e: e.tensor_copy(
                                out=BEXT[:, 62:63], in_=A2[:]), inc=vec)
                            bx_done = vec.n
                            p.wait("tensor", vec, bx_done)
                            p.wait("tensor", act, act.n)
                            p.emit("tensor", lambda e: e.transpose(
                                ps_et[0][0:64, 0:P], BEXT[:, 0:64],
                                self.sb["ident_bf"][:]), inc=ten)
                            bx_used = ten.n
                            p.wait("scalar", ten, ten.n)
                            p.emit("scalar", lambda e, qbl=qbl: e.copy(
                                out=BEXTT[:, qbl * P:(qbl + 1) * P],
                                in_=ps_et[0][0:NBK, 0:P]), inc=act)

                            def make_et(qbl=qbl, ebsl=ebsl, eb0=eb0,
                                        exp_done=exp_done):
                                def go():
                                    et_ev = {0: act.n, 1: act.n}
                                    p.wait("tensor", act, exp_done)
                                    for jt in range(16):
                                        sl = jt % 2
                                        p.wait("tensor", act, et_ev[sl])
                                        p.emit("tensor", lambda e, jt=jt,
                                               sl=sl, eb0=eb0: e.transpose(
                                                   ps_et[sl][:, 0:P],
                                                   EB[:, eb0 + jt * P:
                                                      eb0 + (jt + 1) * P],
                                                   self.sb["ident_bf"][:]),
                                               inc=ten)
                                        ebs_guard[ebsl]["ten"] = ten.n
                                        p.wait("scalar", ten, ten.n)
                                        p.emit("scalar", lambda e, jt=jt,
                                               sl=sl, qbl=qbl: e.copy(
                                                   out=ETS[:, jt * 512
                                                           + qbl * P:
                                                           jt * 512
                                                           + (qbl + 1) * P],
                                                   in_=ps_et[sl][:, 0:P]),
                                               inc=act)
                                        et_ev[sl] = act.n
                                    ebs_guard[ebsl]["vec"] = vec.n
                                    ebs_guard[ebsl]["gq"] = gq.n
                                    ebs_guard[ebsl]["act"] = act.n
                                return go
                            pend_et = make_et()
                        if pend_et is not None:
                            pend_et()
                            pend_et = None
                        ets_done = act.n
                        if not ATTN_LIMIT["av"]:
                            ofu_done = act.n
                            continue
                        p.wait("tensor", act, ets_done)
                        for jt in range(16):
                            p.emit("tensor", lambda e, jt=jt,
                                   first=(jt == 0): e.matmul(
                                       ps_o[:], VT[:, jt * 64:(jt + 1) * 64],
                                       ETS[:, jt * 512:(jt + 1) * 512],
                                       start=first, stop=False))
                        for qbl in range(4):
                            p.emit("tensor", lambda e, qbl=qbl,
                                   last=(qbl == 3): e.matmul(
                                       ps_o[:, qbl * P:(qbl + 1) * P],
                                       BANDW[:],
                                       BEXTT[:, qbl * P:(qbl + 1) * P],
                                       start=False, stop=last),
                                   inc=ten, inc_n=(1 if qbl == 3 else 0))
                        av_done = ten.n
                        p.wait("scalar", ten, av_done)
                        p.emit("scalar", lambda e, qm=qm, hc=hc, hb=hb:
                               e.copy(out=OFU[hb:hb + 64,
                                              hc * NSEQ + qm * 512:
                                              hc * NSEQ + (qm + 1) * 512],
                                      in_=ps_o[:]), inc=act)
                        ofu_done = act.n
                    p.wait("sync", ld, rrow_ld)
                    p.wait("sync", vec, vec.n)
                    p.dma("sync", RREP[:],
                          bass.AP(d["rbuf"], 0, [[0, P], [1, NSEQ]]), sem=ld)
                    rrow_ld = ld.n
                    p.wait("vector", ld, ld.n)
                    p.wait("vector", act, ofu_done)
                    p.emit("vector", lambda e, hc=hc, hb=hb: e.tensor_tensor(
                        out=OFU[hb:hb + 64, hc * NSEQ:(hc + 1) * NSEQ],
                        in0=OFU[hb:hb + 64, hc * NSEQ:(hc + 1) * NSEQ],
                        in1=RREP[hb:hb + 64, :] if hb else RREP[0:64, :],
                        op=ALU.mult), inc=vec)
                    of_scaled = vec.n

                p.wait("tensor", vec, of_scaled)
                ev2_last = vec.n
                for oc in range(CH):
                    sl = oc % 2
                    if oc == 0:
                        load_wproj(0, 0)
                    if oc + 1 < CH:
                        load_wproj(oc + 1, (oc + 1) % 2)
                    elif b == 0:
                        load_wqkv(0, (oc + 1) % 2)
                    p.wait("tensor", gq, wq_ready[sl])
                    for tm in range(4):
                        rsl = tm % 2
                        p.wait("tensor", vec, ev2_last)
                        for c in range(CH):
                            p.emit("tensor", lambda e, c=c, sl=sl, tm=tm:
                                   e.matmul(
                                       ps_sm[:],
                                       WPB[sl][:, c * P:(c + 1) * P],
                                       OFU[:, c * NSEQ + tm * 512:
                                           c * NSEQ + (tm + 1) * 512],
                                       start=(c == 0), stop=(c == CH - 1)),
                                   inc=ten, inc_n=(1 if c == CH - 1 else 0))
                        wq_use[sl] = ten.n
                        mm_done = ten.n
                        p.wait("sync", vec, vec.n)
                        p.wait("sync", gq, gq.n)
                        p.dma("sync", XR[:, rsl * TM:(rsl + 1) * TM],
                              src.ap()[oc * P:(oc + 1) * P,
                                       tb0 + tm * 512:tb0 + (tm + 1) * 512],
                              sem=ld)
                        p.wait("vector", ld, ld.n)
                        p.wait("vector", ten, mm_done)
                        p.emit("vector", lambda e, oc=oc, rsl=rsl:
                               e.scalar_tensor_tensor(
                                   out=OUTT[:, rsl * TM:(rsl + 1) * TM],
                                   in0=ps_sm[:],
                                   scalar=BPJ[:, oc:oc + 1],
                                   in1=XR[:, rsl * TM:(rsl + 1) * TM],
                                   op0=ALU.add, op1=ALU.add), inc=vec)
                        ev2_last = vec.n
                        p.wait("gpsimd", vec, vec.n)
                        p.dma("gpsimd",
                              dst.ap()[oc * P:(oc + 1) * P,
                                       tb0 + tm * 512:tb0 + (tm + 1) * 512],
                              OUTT[:, rsl * TM:(rsl + 1) * TM], sem=gq)
            self.barrier()
            p.flush()

    def phase_mamba(self, src, dst):
        self._phase_prefix()
        nc, p, d = self.nc, self.p, self.dram
        ld, gq, vec, act, ten = self.ld, self.gq, self.vec, self.act, self.ten
        W = 256
        NCH = NSEQ // W
        XCW = W + 6
        p.wait("sync", self.gq, self.gq.n)
        p.wait("sync", self.acc, self.acc.n)
        p.wait("gpsimd", self.ld, self.ld.n)
        with ExitStack() as ctx:
            E = ctx.enter_context
            XT = E(nc.sbuf_tensor(self.u + "m_xt", [P, CH * W], F32))
            UT = E(nc.sbuf_tensor(self.u + "m_ut", [P, CH * W], F32))
            XLB = E(nc.sbuf_tensor(self.u + "m_xlb", [P, CH * W], BF16))
            ST = E(nc.sbuf_tensor(self.u + "m_st", [1, 4 * W], F32))
            MUR = E(nc.sbuf_tensor(self.u + "m_mur", [P, W], F32))
            RSR = E(nc.sbuf_tensor(self.u + "m_rsr", [P, W], F32))
            XC = E(nc.sbuf_tensor(self.u + "m_xc", [P, DIC * XCW], BF16))
            ZS = E(nc.sbuf_tensor(self.u + "m_zs", [P, DIC * W], BF16))
            XBLD = E(nc.sbuf_tensor(self.u + "m_xbld", [P, DIC * W], BF16))
            DTL = E(nc.sbuf_tensor(self.u + "m_dtl", [64, W], BF16))
            BCF = E(nc.sbuf_tensor(self.u + "m_bcf", [96, W], F32))
            DELTA = E(nc.sbuf_tensor(self.u + "m_delta", [P, DIC * W], F32))
            DU = E(nc.sbuf_tensor(self.u + "m_du", [P, DIC * W], BF16))
            DA = E(nc.sbuf_tensor(self.u + "m_da", [P, DST * W], F32))
            DB = E(nc.sbuf_tensor(self.u + "m_db", [P, DST * W], BF16))
            HS = E(nc.sbuf_tensor(self.u + "m_hs", [P, DST * W], F32))
            BREP = E(nc.sbuf_tensor(self.u + "m_brep", [P, DST * W], BF16))
            CREP = E(nc.sbuf_tensor(self.u + "m_crep", [P, DST * W], BF16))
            YT = E(nc.sbuf_tensor(self.u + "m_yt", [P, W], F32))
            Y3 = E(nc.sbuf_tensor(self.u + "m_y3", [P, DIC * W], BF16))
            SP1 = E(nc.sbuf_tensor(self.u + "m_sp1", [P, W], F32))
            SP2 = E(nc.sbuf_tensor(self.u + "m_sp2", [P, W], F32))
            SP3 = E(nc.sbuf_tensor(self.u + "m_sp3", [P, W], F32))
            RES = E(nc.sbuf_tensor(self.u + "m_res", [P, 2 * W], F32))
            STATE = E(nc.sbuf_tensor(self.u + "m_state", [P, DIC * DST], F32))
            CT = E(nc.sbuf_tensor(self.u + "m_ct", [P, 2 * W], F32))
            WINB = [E(nc.sbuf_tensor(self.u + "m_win%d" % i, [P, CH * P], BF16))
                    for i in range(2)]
            WOUTB = [E(nc.sbuf_tensor(self.u + "m_wout%d" % i, [P, DIC * P], BF16))
                     for i in range(2)]
            XPW = E(nc.sbuf_tensor(self.u + "m_xpw", [P, DIC * 96], BF16))
            DTW = E(nc.sbuf_tensor(self.u + "m_dtw", [64, DI], BF16))
            ASB = E(nc.sbuf_tensor(self.u + "m_asb", [P, DIC * DST], F32))
            CW = E(nc.sbuf_tensor(self.u + "m_cw", [P, DIC * 4], F32))
            CB = E(nc.sbuf_tensor(self.u + "m_cb", [P, DIC], F32))
            DTB2 = E(nc.sbuf_tensor(self.u + "m_dtb2", [P, DIC], F32))
            DSK = E(nc.sbuf_tensor(self.u + "m_dsk", [P, DIC], F32))
            ps_a = E(nc.psum_tensor(self.u + "m_psa", [P, W], F32))
            ps_b = E(nc.psum_tensor(self.u + "m_psb", [P, W], F32))
            ps_x = E(nc.psum_tensor(self.u + "m_psx", [96, W], F32))
            ps_sx = E(nc.psum_tensor(self.u + "m_pssx", [1, W], F32))
            ps_sq = E(nc.psum_tensor(self.u + "m_pssq", [1, W], F32))

            p.dma("gpsimd", XPW[:],
                  bass.AP(d["xproj_w"], 0, [[96, P], [96 * P, DIC], [1, 96]]),
                  sem=gq)
            p.dma("gpsimd", DTW[:],
                  bass.AP(d["dtw"], 0, [[DI, 64], [1, DI]]), sem=gq)
            p.dma("sync", ASB[:], d["a_log"].ap(), sem=ld)
            p.dma("sync", CW[:],
                  bass.AP(d["conv_w"], 0, [[4, P], [4 * P, DIC], [1, 4]]),
                  sem=ld)
            p.dma("sync", CB[:], bass.AP(d["conv_b"], 0, [[1, P], [P, DIC]]),
                  sem=ld, allow_slow_non_contiguous=True)
            p.dma("sync", DTB2[:], bass.AP(d["dtb"], 0, [[1, P], [P, DIC]]),
                  sem=ld, allow_slow_non_contiguous=True)
            p.dma("sync", DSK[:], bass.AP(d["d_skip"], 0, [[1, P], [P, DIC]]),
                  sem=ld, allow_slow_non_contiguous=True)
            consts_ld = ld.n
            consts_gq = gq.n
            p.wait("tensor", gq, consts_gq)
            p.wait("scalar", ld, consts_ld)
            p.emit("scalar", lambda e: e.activation(out=ASB[:], in_=ASB[:],
                                                    func=AF.Exp), inc=act)
            p.wait("vector", act, act.n)
            p.emit("vector", lambda e: e.tensor_scalar_mul(
                out=ASB[:], in0=ASB[:], scalar1=-1.0), inc=vec)
            p.emit("scalar", lambda e: e.mul(out=DTB2[:], in_=DTB2[:],
                                             mul=2.0), inc=act)
            asb_ready = vec.n
            dtb2_ready = act.n

            win_ready = [0, 0]
            win_use = [0, 0]
            wout_ready = [0, 0]
            wout_use = [0, 0]
            op_evac = {0: 0, 1: 0}

            def load_win(o, sl):
                p.wait("gpsimd", ten, win_use[sl])
                p.dma("gpsimd", WINB[sl][:],
                      bass.AP(d["w_inproj"], o * P,
                              [[2 * DI, P], [P * 2 * DI, CH], [1, P]]),
                      sem=gq)
                win_ready[sl] = gq.n

            def load_wout(oc, sl):
                p.wait("gpsimd", ten, wout_use[sl])
                p.dma("gpsimd", WOUTB[sl][:],
                      bass.AP(d["w_outproj"], oc * P,
                              [[CD, P], [P * CD, DIC], [1, P]]),
                      sem=gq)
                wout_ready[sl] = gq.n

            for b in range(NB):
                p.wait("vector", ten, ten.n)
                p.wait("vector", act, act.n)
                p.emit("vector", lambda e: e.memset(STATE[:], 0.0), inc=vec)
                p.emit("vector", lambda e: e.memset(
                    bass.AP(XC, 0, [[DIC * XCW, P], [XCW, DIC], [1, 6]]), 0.0),
                    inc=vec)
                for tc in range(NCH):
                    t0 = b * NSEQ + tc * W
                    prev_vec, prev_ten, prev_act = vec.n, ten.n, act.n
                    p.wait("sync", vec, prev_vec)
                    p.wait("sync", ten, prev_ten)
                    p.wait("sync", act, prev_act)
                    for c in range(CH):
                        p.dma("sync", XT[:, c * W:(c + 1) * W],
                              src.ap()[c * P:(c + 1) * P, t0:t0 + W], sem=ld)
                    p.wait("vector", ld, ld.n)
                    srcs = [XT[:, c * W:(c + 1) * W] for c in range(CH)]
                    self.ln_stats(srcs, W, UT, ps_sx[:], ps_sq[:], ST, MUR, RSR)
                    p.wait("vector", ten, ten.n)
                    for c in range(CH):
                        p.emit("vector", lambda e, c=c: e.tensor_sub(
                            out=UT[:, c * W:(c + 1) * W], in0=srcs[c],
                            in1=MUR[:]), inc=vec)
                        p.wait("vector", vec, vec.n)
                        p.emit("vector", lambda e, c=c: e.tensor_tensor(
                            out=XLB[:, c * W:(c + 1) * W],
                            in0=UT[:, c * W:(c + 1) * W], in1=RSR[:],
                            op=ALU.mult), inc=vec)
                    xlb_ready = vec.n

                    p.wait("tensor", vec, xlb_ready)
                    p.wait("tensor", vec, vec.n)
                    evac_done = {0: act.n, 1: act.n}
                    for o in range(2 * DIC):
                        sl = o % 2
                        if b == 0 and tc == 0 and o == 0:
                            load_win(0, 0)
                        if o + 1 < 2 * DIC:
                            load_win(o + 1, (o + 1) % 2)
                        elif tc + 1 < NCH or b + 1 < NB:
                            load_win(0, (o + 1) % 2)
                        ps = ps_a if sl == 0 else ps_b
                        p.wait("tensor", gq, win_ready[sl])
                        p.wait("tensor", act, evac_done[sl])
                        for c in range(CH):
                            p.emit("tensor", lambda e, c=c, sl=sl, ps=ps:
                                   e.matmul(ps[:],
                                            WINB[sl][:, c * P:(c + 1) * P],
                                            XLB[:, c * W:(c + 1) * W],
                                            start=(c == 0),
                                            stop=(c == CH - 1)),
                                   inc=ten, inc_n=(1 if c == CH - 1 else 0))
                        win_use[sl] = ten.n
                        p.wait("scalar", ten, ten.n)
                        if o < DIC:
                            p.emit("scalar", lambda e, o=o, ps=ps: e.copy(
                                out=XC[:, o * XCW + 6:(o + 1) * XCW],
                                in_=ps[:]), inc=act)
                        else:
                            oz = o - DIC
                            p.emit("scalar", lambda e, oz=oz, ps=ps:
                                   e.activation(
                                       out=ZS[:, oz * W:(oz + 1) * W],
                                       in_=ps[:], func=AF.Silu), inc=act)
                        evac_done[sl] = act.n
                    inproj_done = act.n

                    p.wait("vector", act, inproj_done)
                    csil = {0: 0, 1: 0}
                    for dc in range(DIC):
                        base = dc * XCW
                        cs = dc % 2
                        p.wait("vector", act, csil[cs])
                        p.emit("vector", lambda e, base=base, dc=dc, cs=cs:
                               e.tensor_scalar_mul(
                                   out=CT[:, cs * W:(cs + 1) * W],
                                   in0=XC[:, base:base + W],
                                   scalar1=CW[:, dc * 4:dc * 4 + 1]), inc=vec)
                        for k in range(1, 4):
                            p.wait("vector", vec, vec.n)
                            p.emit("vector", lambda e, base=base, dc=dc, k=k,
                                   cs=cs: e.scalar_tensor_tensor(
                                       out=CT[:, cs * W:(cs + 1) * W],
                                       in0=XC[:, base + k:base + k + W],
                                       scalar=CW[:, dc * 4 + k:dc * 4 + k + 1],
                                       in1=CT[:, cs * W:(cs + 1) * W],
                                       op0=ALU.mult, op1=ALU.add),
                                   inc=vec)
                        p.wait("scalar", vec, vec.n)
                        p.emit("scalar", lambda e, dc=dc, cs=cs: e.activation(
                            out=XBLD[:, dc * W:(dc + 1) * W],
                            in_=CT[:, cs * W:(cs + 1) * W],
                            func=AF.Silu, bias=CB[:, dc:dc + 1]), inc=act)
                        csil[cs] = act.n
                    conv_done = act.n
                    p.wait("vector", act, conv_done)
                    p.emit("vector", lambda e: e.tensor_copy(
                        out=bass.AP(XC, 0, [[DIC * XCW, P], [XCW, DIC], [1, 6]]),
                        in_=bass.AP(XC, W, [[DIC * XCW, P], [XCW, DIC], [1, 6]])),
                        inc=vec)

                    p.wait("tensor", act, conv_done)
                    for dc in range(DIC):
                        p.emit("tensor", lambda e, dc=dc: e.matmul(
                            ps_x[:], XPW[:, dc * 96:(dc + 1) * 96],
                            XBLD[:, dc * W:(dc + 1) * W],
                            start=(dc == 0), stop=(dc == DIC - 1)),
                            inc=ten, inc_n=(1 if dc == DIC - 1 else 0))
                    xproj_done = ten.n
                    p.wait("scalar", ten, xproj_done)
                    p.emit("scalar", lambda e: e.copy(out=DTL[:],
                                                      in_=ps_x[0:64, :]))
                    p.emit("scalar", lambda e: e.copy(out=BCF[64:96, :],
                                                      in_=ps_x[64:96, :]),
                           inc=act)
                    bcf_done = act.n
                    p.wait("gpsimd", act, bcf_done)
                    p.wait("gpsimd", vec, vec.n)
                    p.wait("gpsimd", gq, gq.n)
                    p.dma("gpsimd",
                          bass.AP(d["bcbuf"], tc * W, [[NSEQ, 32], [1, W]]),
                          BCF[64:96, :], sem=gq)
                    bc_written = gq.n
                    p.wait("gpsimd", gq, bc_written)
                    p.dma("gpsimd", BREP[:],
                          bass.AP(d["bcbuf"], tc * W,
                                  [[0, P], [NSEQ, DST], [1, W]]), sem=gq)
                    p.dma("gpsimd", CREP[:],
                          bass.AP(d["bcbuf"], DST * NSEQ + tc * W,
                                  [[0, P], [NSEQ, DST], [1, W]]), sem=gq)
                    brep_ready = gq.n

                    p.wait("tensor", act, act.n)
                    dt_evac = {0: act.n, 1: act.n}
                    for dc in range(DIC):
                        sl = dc % 2
                        ps = ps_a if sl == 0 else ps_b
                        p.wait("tensor", act, dt_evac[sl])
                        p.emit("tensor", lambda e, dc=dc, ps=ps: e.matmul(
                            ps[:], DTW[:, dc * P:(dc + 1) * P], DTL[:],
                            start=True, stop=True), inc=ten)
                        p.wait("scalar", ten, ten.n)
                        p.emit("scalar", lambda e, dc=dc, ps=ps: e.activation(
                            out=SP1[:], in_=ps[:], func=AF.Identity,
                            bias=DTB2[:, dc:dc + 1]), inc=act)
                        dt_evac[sl] = act.n
                        p.wait("vector", act, act.n)
                        p.emit("vector", lambda e: e.tensor_scalar_max(
                            out=SP2[:], in0=SP1[:], scalar1=0.0), inc=vec)
                        p.wait("scalar", vec, vec.n)
                        p.emit("scalar", lambda e: e.activation(
                            out=SP3[:], in_=SP1[:], func=AF.Abs), inc=act)
                        p.wait("scalar", act, act.n)
                        p.emit("scalar", lambda e: e.activation(
                            out=SP3[:], in_=SP3[:], func=AF.Exp, scale=-1.0),
                            inc=act)
                        p.wait("scalar", act, act.n)
                        p.emit("scalar", lambda e: e.activation(
                            out=SP3[:], in_=SP3[:], func=AF.Ln,
                            bias=self.sb["c1"][:]), inc=act)
                        p.wait("vector", act, act.n)
                        p.emit("vector", lambda e, dc=dc: e.tensor_add(
                            out=DELTA[:, dc * W:(dc + 1) * W], in0=SP2[:],
                            in1=SP3[:]))
                        p.emit("vector", lambda e, dc=dc: e.tensor_tensor(
                            out=DU[:, dc * W:(dc + 1) * W],
                            in0=DELTA[:, dc * W:(dc + 1) * W],
                            in1=XBLD[:, dc * W:(dc + 1) * W], op=ALU.mult),
                            inc=vec)
                    delta_done = vec.n

                    p.wait("scalar", vec, delta_done)
                    p.wait("scalar", vec, asb_ready)
                    p.wait("vector", gq, brep_ready)
                    p.wait("vector", ten, ten.n)
                    for dc in range(DIC):
                        p.wait("scalar", vec, vec.n)
                        for n in range(DST):
                            p.emit("scalar", lambda e, dc=dc, n=n:
                                   e.activation(
                                       out=DA[:, n * W:(n + 1) * W],
                                       in_=DELTA[:, dc * W:(dc + 1) * W],
                                       func=AF.Exp,
                                       scale=ASB[:, dc * DST + n:
                                                 dc * DST + n + 1]),
                                   inc=act, inc_n=(1 if n == DST - 1 else 0))
                        da_done = act.n
                        p.wait("vector", act, da_done)
                        p.emit("vector", lambda e, dc=dc: e.tensor_tensor(
                            out=DB[:].rearrange("p (n w) -> p n w", n=DST),
                            in0=DU[:, dc * W:(dc + 1) * W].unsqueeze(1)
                            .broadcast_to((P, DST, W)),
                            in1=BREP[:].rearrange("p (n w) -> p n w", n=DST),
                            op=ALU.mult), inc=vec)
                        p.wait("vector", vec, vec.n)
                        for n in range(DST):
                            p.emit("vector", lambda e, dc=dc, n=n:
                                   e.tensor_tensor_scan(
                                       out=HS[:, n * W:(n + 1) * W],
                                       data0=DA[:, n * W:(n + 1) * W],
                                       data1=DB[:, n * W:(n + 1) * W],
                                       initial=STATE[:, dc * DST + n:
                                                     dc * DST + n + 1],
                                       op0=ALU.mult, op1=ALU.add), inc=vec)
                        p.wait("vector", vec, vec.n)
                        p.emit("vector", lambda e, dc=dc: e.tensor_copy(
                            out=STATE[:, dc * DST:(dc + 1) * DST],
                            in_=bass.AP(HS, W - 1, [[DST * W, P], [W, DST]])),
                            inc=vec)
                        p.wait("vector", vec, vec.n)
                        p.emit("vector", lambda e: e.tensor_tensor(
                            out=DA[:], in0=HS[:], in1=CREP[:], op=ALU.mult),
                            inc=vec)
                        p.wait("vector", vec, vec.n)
                        p.emit("vector", lambda e: e.tensor_reduce(
                            out=YT[:],
                            in_=bass.AP(DA, 0, [[DST * W, P], [1, W], [W, DST]]),
                            axis=AX.X, op=ALU.add), inc=vec)
                        p.wait("vector", vec, vec.n)
                        p.emit("vector", lambda e, dc=dc:
                               e.scalar_tensor_tensor(
                                   out=YT[:],
                                   in0=XBLD[:, dc * W:(dc + 1) * W],
                                   scalar=DSK[:, dc:dc + 1], in1=YT[:],
                                   op0=ALU.mult, op1=ALU.add), inc=vec)
                        p.wait("vector", vec, vec.n)
                        p.emit("vector", lambda e, dc=dc: e.tensor_tensor(
                            out=Y3[:, dc * W:(dc + 1) * W], in0=YT[:],
                            in1=ZS[:, dc * W:(dc + 1) * W], op=ALU.mult),
                            inc=vec)
                    scan_done = vec.n

                    p.wait("tensor", vec, scan_done)
                    p.wait("tensor", act, act.n)
                    for oc in range(CH):
                        sl = oc % 2
                        if b == 0 and tc == 0 and oc == 0:
                            load_wout(0, 0)
                        if oc + 1 < CH:
                            load_wout(oc + 1, (oc + 1) % 2)
                        elif tc + 1 < NCH or b + 1 < NB:
                            load_wout(0, (oc + 1) % 2)
                        ps = ps_a if sl == 0 else ps_b
                        p.wait("tensor", gq, wout_ready[sl])
                        p.wait("tensor", vec, op_evac[sl])
                        for dc in range(DIC):
                            p.emit("tensor", lambda e, dc=dc, sl=sl, ps=ps:
                                   e.matmul(ps[:],
                                            WOUTB[sl][:, dc * P:(dc + 1) * P],
                                            Y3[:, dc * W:(dc + 1) * W],
                                            start=(dc == 0),
                                            stop=(dc == DIC - 1)),
                                   inc=ten, inc_n=(1 if dc == DIC - 1 else 0))
                        wout_use[sl] = ten.n
                        rsl = oc % 2
                        p.wait("sync", vec, vec.n)
                        p.wait("sync", gq, gq.n)
                        p.dma("sync", RES[:, rsl * W:(rsl + 1) * W],
                              src.ap()[oc * P:(oc + 1) * P, t0:t0 + W],
                              sem=ld)
                        p.wait("vector", ld, ld.n)
                        p.wait("vector", ten, ten.n)
                        p.emit("vector", lambda e, rsl=rsl, ps=ps:
                               e.tensor_add(
                                   out=RES[:, rsl * W:(rsl + 1) * W],
                                   in0=RES[:, rsl * W:(rsl + 1) * W],
                                   in1=ps[:]), inc=vec)
                        op_evac[sl] = vec.n
                        p.wait("gpsimd", vec, vec.n)
                        p.dma("gpsimd",
                              dst.ap()[oc * P:(oc + 1) * P, t0:t0 + W],
                              RES[:, rsl * W:(rsl + 1) * W], sem=gq)
            self.barrier()
            p.flush()

    def finish(self):
        p = self.p
        p.wait("sync", self.gq, self.gq.n)
        p.wait("sync", self.ld, self.ld.n)
        p.wait("sync", self.acc, self.acc.n)
        self.barrier()
        p.flush()
        self.ctx.close()


def build_program(debug=False, do_attn=True, do_ffn1=True, do_mamba=True,
                  do_ffn2=True):
    nc = bass.Bass("TRN2", target_bir_lowering=False)
    b = Builder(nc, debug=debug)
    b.declare_dram()
    b.alloc_common()
    d = b.dram
    cur = d["x_f"]
    if do_attn:
        b.phase_attn(cur, d["r1"])
        cur = d["r1"]
    if do_ffn1:
        nxt = d["r2"] if (do_mamba or do_ffn2) else d["y_f"]
        b.phase_ffn(cur, nxt)
        cur = nxt
    if do_mamba:
        b.phase_mamba(cur, d["r1"])
        cur = d["r1"]
    if do_ffn2:
        b.phase_ffn(cur, d["y_f"])
        cur = d["y_f"]
    if cur is not d["y_f"]:
        b.copy_dram(cur, d["y_f"])
    b.finish()
    return nc


def _copy_dram(self, src, dst):
    nc, p = self.nc, self.p
    self._phase_prefix()
    with ExitStack() as ctx:
        tile = ctx.enter_context(nc.sbuf_tensor(self.u + "cp_t", [P, CH * TM], F32))
        for tm in range(NTM):
            t0 = tm * TM
            p.wait("sync", self.gq, self.gq.n)
            for c in range(CH):
                p.dma("sync", tile[:, c * TM:(c + 1) * TM],
                      src.ap()[c * P:(c + 1) * P, t0:t0 + TM], sem=self.ld)
            p.wait("gpsimd", self.ld, self.ld.n)
            for c in range(CH):
                p.dma("gpsimd", dst.ap()[c * P:(c + 1) * P, t0:t0 + TM],
                      tile[:, c * TM:(c + 1) * TM], sem=self.gq)
        self.barrier()
        p.flush()


Builder.copy_dram = _copy_dram



def _mask3():
    qq = np.arange(P)[:, None]
    jj = np.arange(P)[None, :]
    parts = [((jj - qq + c) >= 31).astype(np.float32) for c in (-P, 0, P)]
    return np.ascontiguousarray(np.concatenate(parts, axis=1))


def _prep_inputs(kw):
    x = np.asarray(kw["L0_attn_a_x"])
    base = {
        "wqkv_t": np.asarray(kw["L0_attn_a_w_qkv"]).T,
        "wproj_t": np.asarray(kw["L0_attn_a_w_proj"]).T,
        "b_proj": kw["L0_attn_a_b_proj"],
        "rpe_q": kw["L0_attn_a_rpe_q_table"],
        "rpe_k": kw["L0_attn_a_rpe_k_table"],
        "rpe_v": kw["L0_attn_a_rpe_v_table"],
        "rpe_v_t": np.asarray(kw["L0_attn_a_rpe_v_table"]).transpose(0, 2, 1),
        "w_inproj": kw["L0_attn_b_in_proj_w"],
        "conv_w": kw["L0_attn_b_conv_w"],
        "conv_b": kw["L0_attn_b_conv_b"],
        "xproj_w": kw["L0_attn_b_x_proj_w"],
        "dtw": kw["L0_attn_b_dt_proj_w"],
        "dtb": kw["L0_attn_b_dt_proj_b"],
        "a_log": np.asarray(kw["L0_attn_b_A_log"]).reshape(DIC, P, DST)
        .transpose(1, 0, 2).reshape(P, DIC * DST),
        "d_skip": kw["L0_attn_b_D_skip"],
        "w_outproj": kw["L0_attn_b_out_proj_w"],
        "ln_w": kw["L0_ffn_ln_w"],
        "ln_b": kw["L0_ffn_ln_b"],
        "router_w": kw["L0_ffn_router_w"],
        "gate_w": kw["L0_ffn_gate_w"],
        "up_w": kw["L0_ffn_up_w"],
        "down_w": kw["L0_ffn_down_w"],
        "sh_gate": kw["L0_ffn_sh_gate"],
        "sh_up": kw["L0_ffn_sh_up"],
        "sh_down": kw["L0_ffn_sh_down"],
        "ident": np.eye(P, dtype=np.float32),
        "mask3": _mask3(),
    }
    base = {k: np.ascontiguousarray(np.asarray(v), dtype=np.float32)
            for k, v in base.items()}
    maps = []
    for core in range(8):
        xf = np.ascontiguousarray(
            np.asarray(x[2 * core:2 * core + 2]).reshape(T, CD).T,
            dtype=np.float32)
        m = dict(base)
        m["x_f"] = xf
        maps.append(m)
    return maps


def kernel(**inputs):
    maps = _prep_inputs(inputs)
    nc = build_program()
    res = run_bass_kernel_spmd(nc, maps, core_ids=list(range(8)))
    out = np.empty((16, NSEQ, CD), np.float32)
    for core in range(8):
        yf = np.asarray(res.results[core]["y_f"])
        out[2 * core:2 * core + 2] = yf.T.reshape(NB, NSEQ, CD)
    return out


# version 06
# speedup vs baseline: 1.1147x; 1.1147x over previous
from contextlib import ExitStack

import numpy as np
import concourse.bass as bass
import concourse.mybir as mybir
from concourse.bass_utils import run_bass_kernel_spmd

F32 = mybir.dt.float32
BF16 = mybir.dt.bfloat16
AF = mybir.ActivationFunctionType
ALU = mybir.AluOpType
AX = mybir.AxisListType

P = 128
CD = 1024
CH = CD // P
NB = 2
NSEQ = 2048
T = NB * NSEQ
NH = 16
HD = 64
SCALE = HD ** -0.5
NBK = 63
DI = 2048
DIC = DI // P
DST = 16
DTR = 64
NE = 8
IF = 2048
EPS = 1e-5
TM = 512
NTM = T // TM

ENGINES = ("sync", "gpsimd", "vector", "scalar", "tensor")

STRICT_ENGINES = False

ATTN_LIMIT = {"heads": NH, "score": True, "av": True, "prep": True,
              "aug": True, "vt": True, "dq": True, "dk": True}


class Sem:
    __slots__ = ("h", "n", "name")

    def __init__(self, nc, name):
        self.name = name
        self.h = nc.semaphore(name).__enter__()
        self.n = 0


class Prog:

    def __init__(self, nc):
        self.nc = nc
        self.streams = {e: [] for e in ENGINES}
        self.qeng = {}
        self.relay = {}
        self.relay_cache = {}
        self.strict = {}

    def register_queue(self, sem, eng):
        self.qeng[sem.name] = eng
        self.relay[sem.name] = Sem(self.nc, sem.name + "_r")

    def on(self, eng, fn):
        self.streams[eng].append(fn)

    def wait(self, eng, sem, val=None):
        v = sem.n if val is None else val
        if v <= 0:
            return
        qe = self.qeng.get(sem.name)
        if qe is None or qe == eng:
            self.on(eng, lambda e, s=sem.h, v=v: e.wait_ge(s, v))
            return
        rs = self.relay[sem.name]
        cached = self.relay_cache.get(sem.name)
        if cached is None or cached[0] < v:
            qn = sem.n
            self.on(qe, lambda e, s=sem.h, qn=qn: e.wait_ge(s, qn))
            rs.n += 1
            self.on(qe, lambda e, s=rs.h: e.nop().then_inc(s, 1))
            self.relay_cache[sem.name] = (qn, rs.n)
            cached = (qn, rs.n)
        self.on(eng, lambda e, s=rs.h, v=cached[1]: e.wait_ge(s, v))

    def emit(self, eng, fn, inc=None, inc_n=1):
        strict = self.strict.get(eng)
        if strict is not None:
            if strict.n > 0:
                self.on(eng, lambda e, s=strict.h, v=strict.n: e.wait_ge(s, v))
            if inc is None or inc_n == 0:
                inc, inc_n = strict, 1
        if inc is not None and inc_n > 0:
            inc.n += inc_n

            def go(e, fn=fn, s=inc.h, k=inc_n):
                fn(e).then_inc(s, k)
        else:
            def go(e, fn=fn):
                fn(e)
        self.on(eng, go)

    def dma(self, eng, dst, src, sem=None, **kw):
        if sem is not None:
            sem.n += 16
            if sem.name in self.relay_cache:
                pass

            def go(e, dst=dst, src=src, s=sem.h, kw=kw):
                e.dma_start(out=dst, in_=src, **kw).then_inc(s, 16)
        else:
            def go(e, dst=dst, src=src, kw=kw):
                e.dma_start(out=dst, in_=src, **kw)
        self.on(eng, go)

    def flush(self):
        streams = self.streams
        self.streams = {e: [] for e in ENGINES}
        with self.nc.Block() as block:
            for eng in ENGINES:
                fns = streams[eng]

                def section(e, fns=fns):
                    for f in fns:
                        f(e)
                getattr(block, eng)(section)


class Builder:
    def __init__(self, nc, debug=False):
        self.nc = nc
        self.debug = debug
        self.p = Prog(nc)
        self.ctx = ExitStack()
        self.ld = Sem(nc, "s_ld")
        self.gq = Sem(nc, "s_gq")
        self.acc = Sem(nc, "s_acc")
        self.vec = Sem(nc, "s_vec")
        self.act = Sem(nc, "s_act")
        self.ten = Sem(nc, "s_ten")
        self.bar = Sem(nc, "s_bar")
        self.p.register_queue(self.ld, "sync")
        self.p.register_queue(self.gq, "gpsimd")
        self.p.register_queue(self.acc, "gpsimd")
        if STRICT_ENGINES:
            self.p.strict["vector"] = self.vec
            self.p.strict["scalar"] = self.act
        self.dram = {}
        self.sb = {}
        self.u = ""
        self._uid = 0

    def _phase_prefix(self):
        self._uid += 1
        self.u = "p%d_" % self._uid

    def declare_dram(self):
        nc = self.nc
        d = self.dram

        def din(name, shape):
            d[name] = nc.dram_tensor(name, list(shape), F32,
                                     kind="ExternalInput")

        din("x_f", (CD, T))
        din("wqkv_t", (CD, 3 * CD))
        din("wproj_t", (CD, CD))
        din("b_proj", (CD,))
        din("rpe_q", (NH, HD, NBK))
        din("rpe_k", (NH, HD, NBK))
        din("rpe_v", (NH, NBK, HD))
        din("rpe_v_t", (NH, HD, NBK))
        din("w_inproj", (CD, 2 * DI))
        din("conv_w", (DI, 4))
        din("conv_b", (DI,))
        din("xproj_w", (DI, DTR + 2 * DST))
        din("dtw", (DTR, DI))
        din("dtb", (DI,))
        din("a_log", (P, DIC * DST))
        din("d_skip", (DI,))
        din("w_outproj", (DI, CD))
        din("ln_w", (CD,))
        din("ln_b", (CD,))
        din("router_w", (CD, NE))
        din("gate_w", (NE, CD, IF))
        din("up_w", (NE, CD, IF))
        din("down_w", (NE, IF, CD))
        din("sh_gate", (CD, IF))
        din("sh_up", (CD, IF))
        din("sh_down", (IF, CD))
        din("ident", (P, P))
        din("mask3", (P, 3 * P))

        d["y_f"] = nc.dram_tensor("y_f", [CD, T], F32, kind="ExternalOutput")
        okind = "ExternalOutput" if self.debug else "Internal"
        d["r1"] = nc.dram_tensor("r1", [CD, T], F32, kind=okind)
        d["r2"] = nc.dram_tensor("r2", [CD, T], F32, kind=okind)
        d["u_buf"] = nc.dram_tensor("u_buf", [CD, T], F32, kind=okind)
        d["moe_acc"] = nc.dram_tensor("moe_acc", [CD, T], F32, kind=okind)
        d["statbuf"] = nc.dram_tensor("statbuf", [2 * T], F32, kind="Internal")
        d["mbuf"] = nc.dram_tensor("mbuf", [NE * T], F32, kind=okind)
        d["nbuf"] = nc.dram_tensor("nbuf", [CD, T], BF16, kind=okind)
        d["bcbuf"] = nc.dram_tensor("bcbuf", [2 * DST * NSEQ], F32,
                                    kind=okind)
        d["pqbuf"] = nc.dram_tensor("pqbuf", [16 * P * 512], BF16,
                                    kind="Internal")
        d["pkbuf"] = nc.dram_tensor("pkbuf", [16 * P * 512], BF16,
                                    kind="Internal")
        d["ebuf"] = nc.dram_tensor("ebuf", [P * 640], BF16, kind="Internal")
        d["rbuf"] = nc.dram_tensor("rbuf", [NSEQ], F32, kind="Internal")
        d["qkvbuf"] = nc.dram_tensor("qkvbuf", [3 * CD * NSEQ], BF16,
                                     kind="Internal")

    def alloc_common(self):
        E = self.ctx.enter_context
        nc, p = self.nc, self.p
        sb = self.sb
        sb["ones_bf"] = E(nc.sbuf_tensor(self.u + "c_ones", [P, 1], BF16))
        sb["ones_f"] = E(nc.sbuf_tensor(self.u + "c_onesf", [P, 1], F32))
        sb["ident_bf"] = E(nc.sbuf_tensor(self.u + "c_idb", [P, P], BF16))
        sb["ident_f"] = E(nc.sbuf_tensor(self.u + "c_idf", [P, P], F32))
        sb["ceps"] = E(nc.sbuf_tensor(self.u + "c_eps", [1, 1], F32))
        sb["c1"] = E(nc.sbuf_tensor(self.u + "c_one", [P, 1], F32))
        p.emit("vector", lambda e: e.memset(sb["ones_bf"][:], 1.0))
        p.emit("vector", lambda e: e.memset(sb["ones_f"][:], 1.0))
        p.emit("vector", lambda e: e.memset(sb["ceps"][:], EPS))
        p.emit("vector", lambda e: e.memset(sb["c1"][:], 1.0), inc=self.vec)
        p.dma("sync", sb["ident_f"][:], self.dram["ident"].ap(), sem=self.ld)
        p.dma("gpsimd", sb["ident_bf"][:], self.dram["ident"].ap(),
              sem=self.gq)

    def barrier(self):
        p, bar = self.p, self.bar
        bar.n += len(ENGINES)
        tgt = bar.n
        for eng in ENGINES:
            p.on(eng, lambda e, s=bar.h: e.nop().then_inc(s, 1))
            p.on(eng, lambda e, s=bar.h, v=tgt: e.wait_ge(s, v))

    def ln_stats(self, srcs, W, sqf, ps_sx, ps_sq, st, mur, rsr):
        p = self.p
        vec, act, ten, ld = self.vec, self.act, self.ten, self.ld
        onesf = self.sb["ones_f"]
        p.wait("scalar", vec, vec.n)
        p.wait("scalar", ld, ld.n)
        for c in range(CH):
            p.emit("scalar", lambda e, c=c: e.activation(
                out=sqf[:, c * W:(c + 1) * W], in_=srcs[c],
                func=AF.Square), inc=act, inc_n=(1 if c == CH - 1 else 0))
        sq_done = act.n
        p.wait("tensor", act, sq_done)
        for c in range(CH):
            p.emit("tensor", lambda e, c=c: e.matmul(
                ps_sx, onesf[:], srcs[c],
                start=(c == 0), stop=(c == CH - 1)))
            p.emit("tensor", lambda e, c=c: e.matmul(
                ps_sq, onesf[:], sqf[:, c * W:(c + 1) * W],
                start=(c == 0), stop=(c == CH - 1)),
                inc=ten, inc_n=(1 if c == CH - 1 else 0))
        mm_done = ten.n
        mu, ex2, tmp, rrow = (st[:, 0:W], st[:, W:2 * W], st[:, 2 * W:3 * W],
                              st[:, 3 * W:4 * W])
        p.wait("scalar", ten, mm_done)
        p.emit("scalar", lambda e: e.mul(out=mu, in_=ps_sx, mul=1.0 / CD))
        p.emit("scalar", lambda e: e.mul(out=ex2, in_=ps_sq, mul=1.0 / CD),
               inc=act)
        p.wait("vector", act, act.n)
        p.emit("vector", lambda e: e.tensor_tensor(out=tmp, in0=mu, in1=mu,
                                                   op=ALU.mult), inc=vec)
        p.wait("vector", vec, vec.n)
        p.emit("vector", lambda e: e.tensor_sub(out=tmp, in0=ex2, in1=tmp),
               inc=vec)
        p.wait("scalar", self.vec, self.vec.n)
        p.emit("scalar", lambda e: e.activation(out=tmp, in_=tmp, func=AF.Sqrt,
                                                bias=self.sb["ceps"][:]),
               inc=act)
        p.wait("vector", act, act.n)
        p.emit("vector", lambda e: e.reciprocal(out=rrow, in_=tmp), inc=vec)
        p.wait("sync", vec, vec.n)
        p.wait("sync", ld, ld.n)
        statbuf = self.dram["statbuf"]
        p.dma("sync", bass.AP(statbuf, 0, [[1, 1], [1, W]]), mu, sem=ld)
        p.dma("sync", bass.AP(statbuf, T, [[1, 1], [1, W]]), rrow, sem=ld)
        p.wait("sync", ld, ld.n)
        p.dma("sync", mur[:, 0:W], bass.AP(statbuf, 0, [[0, P], [1, W]]),
              sem=ld)
        p.dma("sync", rsr[:, 0:W], bass.AP(statbuf, T, [[0, P], [1, W]]),
              sem=ld)
        p.wait("vector", ld, ld.n)

    def phase_ffn(self, src, dst):
        self._phase_prefix()
        nc, p, d = self.nc, self.p, self.dram
        ld, gq, acc, vec, act, ten = (self.ld, self.gq, self.acc, self.vec,
                                      self.act, self.ten)
        p.wait("sync", self.gq, self.gq.n)
        p.wait("sync", self.acc, self.acc.n)
        p.wait("gpsimd", self.ld, self.ld.n)
        with ExitStack() as ctx:
            E = ctx.enter_context
            XT = E(nc.sbuf_tensor(self.u + "f_xt", [P, CH * TM], F32))
            UT = E(nc.sbuf_tensor(self.u + "f_ut", [P, CH * TM], F32))
            XB = E(nc.sbuf_tensor(self.u + "f_xb", [P, CH * TM], BF16))
            TSC = E(nc.sbuf_tensor(self.u + "f_tsc", [P, TM], F32))
            NT = E(nc.sbuf_tensor(self.u + "f_nt", [P, 2 * CH * TM], BF16))
            ST = E(nc.sbuf_tensor(self.u + "f_st", [1, 4 * TM], F32))
            MUR = E(nc.sbuf_tensor(self.u + "f_mur", [P, TM], F32))
            RSR = E(nc.sbuf_tensor(self.u + "f_rsr", [P, TM], F32))
            LNW = E(nc.sbuf_tensor(self.u + "f_lnw", [P, CH], F32))
            LNB = E(nc.sbuf_tensor(self.u + "f_lnb", [P, CH], F32))
            RW = E(nc.sbuf_tensor(self.u + "f_rw", [P, CH * NE], F32))
            MSK = E(nc.sbuf_tensor(self.u + "f_msk", [P, NE], F32))
            MROW = E(nc.sbuf_tensor(self.u + "f_mrow", [P, 1], F32))
            SGT = E(nc.sbuf_tensor(self.u + "f_sgt", [P, 1], F32))
            MT = E(nc.sbuf_tensor(self.u + "f_mt", [NE, P], F32))
            GW = [E(nc.sbuf_tensor(self.u + "f_gw%d" % i, [P, 8 * 1024], BF16))
                  for i in range(2)]
            UW = [E(nc.sbuf_tensor(self.u + "f_uw%d" % i, [P, 8 * 1024], BF16))
                  for i in range(2)]
            DW = [E(nc.sbuf_tensor(self.u + "f_dw%d" % i, [P, 8 * 1024], BF16))
                  for i in range(2)]
            XE = E(nc.sbuf_tensor(self.u + "f_xe", [P, CH * TM], BF16))
            SQ = XE
            HM = E(nc.sbuf_tensor(self.u + "f_hm", [P, 8 * TM], BF16))
            SG = E(nc.sbuf_tensor(self.u + "f_sg", [P, 2 * TM], BF16))
            WREP = E(nc.sbuf_tensor(self.u + "f_wrep", [P, 2 * TM], F32))
            DEV = E(nc.sbuf_tensor(self.u + "f_dev", [P, 4 * TM], F32))

            with ExitStack() as ctx2:
                E2 = ctx2.enter_context
                ps_sx = E2(nc.psum_tensor(self.u + "f_pssx", [1, TM], F32))
                ps_sq = E2(nc.psum_tensor(self.u + "f_pssq", [1, TM], F32))
                ps_lg = E2(nc.psum_tensor(self.u + "f_pslg", [P, NE], F32))
                ps_mt = E2(nc.psum_tensor(self.u + "f_psmt", [NE, P], F32))

                p.dma("sync", LNW[:], bass.AP(d["ln_w"], 0, [[1, P], [P, CH]]),
                      sem=ld, allow_slow_non_contiguous=True)
                p.dma("sync", LNB[:], bass.AP(d["ln_b"], 0, [[1, P], [P, CH]]),
                      sem=ld, allow_slow_non_contiguous=True)
                p.dma("gpsimd", RW[:],
                      bass.AP(d["router_w"], 0, [[NE, P], [P * NE, CH], [1, NE]]),
                      sem=gq)
                consts_ld = ld.n
                consts_gq = gq.n

                router_ten_use = 0
                router_act_use = 0
                for tm in range(NTM):
                    t0 = tm * TM
                    prev_ten = ten.n
                    prev_vec = vec.n
                    prev_act = act.n
                    prev_gq = gq.n
                    p.wait("sync", vec, prev_vec)
                    p.wait("sync", ten, prev_ten)
                    p.wait("sync", act, prev_act)
                    for c in range(CH):
                        p.dma("sync", XT[:, c * TM:(c + 1) * TM],
                              src.ap()[c * P:(c + 1) * P, t0:t0 + TM], sem=ld)
                    p.wait("vector", ld, ld.n)
                    p.wait("vector", ten, prev_ten)
                    srcs = [XT[:, c * TM:(c + 1) * TM] for c in range(CH)]
                    p.wait("scalar", gq, prev_gq)
                    self.ln_stats(srcs, TM, UT, ps_sx[:], ps_sq[:], ST, MUR, RSR)
                    p.wait("vector", ten, ten.n)
                    p.wait("vector", gq, prev_gq)
                    for c in range(CH):
                        p.emit("vector", lambda e, c=c: e.tensor_sub(
                            out=UT[:, c * TM:(c + 1) * TM], in0=srcs[c],
                            in1=MUR[:]), inc=vec)
                        p.wait("vector", vec, vec.n)
                        p.emit("vector", lambda e, c=c: e.tensor_tensor(
                            out=UT[:, c * TM:(c + 1) * TM],
                            in0=UT[:, c * TM:(c + 1) * TM], in1=RSR[:],
                            op=ALU.mult), inc=vec)
                    u_done = vec.n
                    p.wait("gpsimd", vec, u_done)
                    for c in range(CH):
                        p.dma("gpsimd", d["u_buf"].ap()[c * P:(c + 1) * P,
                                                        t0:t0 + TM],
                              UT[:, c * TM:(c + 1) * TM], sem=gq)
                    usrc = [UT[:, c * TM:(c + 1) * TM] for c in range(CH)]
                    self.ln_stats(usrc, TM, XT, ps_sx[:], ps_sq[:], ST, MUR, RSR)
                    p.wait("vector", ld, consts_ld)
                    p.wait("vector", ten, ten.n)
                    for c in range(CH):
                        p.emit("vector", lambda e, c=c: e.tensor_sub(
                            out=XT[:, c * TM:(c + 1) * TM], in0=usrc[c],
                            in1=MUR[:]), inc=vec)
                        p.wait("vector", vec, vec.n)
                        p.emit("vector", lambda e, c=c: e.tensor_tensor(
                            out=XT[:, c * TM:(c + 1) * TM],
                            in0=XT[:, c * TM:(c + 1) * TM], in1=RSR[:],
                            op=ALU.mult), inc=vec)
                        p.wait("vector", vec, vec.n)
                        p.emit("vector", lambda e, c=c: e.tensor_scalar(
                            out=XT[:, c * TM:(c + 1) * TM],
                            in0=XT[:, c * TM:(c + 1) * TM],
                            scalar1=LNW[:, c:c + 1], scalar2=LNB[:, c:c + 1],
                            op0=ALU.mult, op1=ALU.add), inc=vec)
                        p.wait("vector", vec, vec.n)
                        p.emit("vector", lambda e, c=c: e.tensor_copy(
                            out=XB[:, c * TM:(c + 1) * TM],
                            in_=XT[:, c * TM:(c + 1) * TM]), inc=vec)
                    norm_done = vec.n
                    p.wait("gpsimd", vec, norm_done)
                    p.dma("gpsimd",
                          bass.AP(d["nbuf"], t0, [[T, P], [T * P, CH], [1, TM]]),
                          XB[:], sem=gq)
                    p.wait("tensor", vec, norm_done)
                    p.wait("tensor", gq, consts_gq)
                    for tb in range(TM // P):
                        p.wait("tensor", vec, vec.n)
                        p.wait("tensor", act, router_act_use)
                        for c in range(CH):
                            p.emit("tensor", lambda e, c=c, tb=tb: e.matmul(
                                ps_lg[:],
                                XT[:, c * TM + tb * P:c * TM + (tb + 1) * P],
                                RW[:, c * NE:(c + 1) * NE],
                                start=(c == 0), stop=(c == CH - 1)),
                                inc=ten, inc_n=(1 if c == CH - 1 else 0))
                        router_ten_use = ten.n
                        p.wait("vector", ten, ten.n)
                        p.emit("vector", lambda e: e.tensor_reduce(
                            out=MROW[:], in_=ps_lg[:], axis=AX.X, op=ALU.max),
                            inc=vec)
                        p.wait("scalar", vec, vec.n)
                        p.emit("scalar", lambda e: e.activation(
                            out=SGT[:], in_=MROW[:], func=AF.Sigmoid), inc=act)
                        p.wait("vector", act, act.n)
                        p.emit("vector", lambda e: e.tensor_scalar(
                            out=MSK[:], in0=ps_lg[:], scalar1=MROW[:],
                            scalar2=SGT[:], op0=ALU.is_equal, op1=ALU.mult),
                            inc=vec)
                        p.wait("tensor", vec, vec.n)
                        p.emit("tensor", lambda e: e.transpose(
                            ps_mt[:], MSK[:], self.sb["ident_f"][:]), inc=ten)
                        router_ten_use = ten.n
                        p.wait("scalar", ten, ten.n)
                        p.emit("scalar", lambda e: e.copy(out=MT[:],
                                                          in_=ps_mt[:]),
                               inc=act)
                        router_act_use = act.n
                        p.wait("gpsimd", act, act.n)
                        p.dma("gpsimd",
                              bass.AP(d["mbuf"], t0 + tb * P,
                                      [[T, NE], [1, P]]),
                              MT[:], sem=gq)
                    prev_ten = router_ten_use
                mbuf_done = gq.n
                self.barrier()

            ps_g = [E(nc.psum_tensor(self.u + "f_psg%d" % i, [P, TM], F32))
                    for i in range(2)]
            ps_u = [E(nc.psum_tensor(self.u + "f_psu%d" % i, [P, TM], F32))
                    for i in range(2)]
            ps_d = [E(nc.psum_tensor(self.u + "f_psd%d" % i, [P, TM], F32))
                    for i in range(2)]

            acc0 = acc.n
            wready = [0, 0]
            wlast_use = [0, 0]
            silu_done = [0, 0]
            hm_written = [0, 0]
            xe_last_use = 0
            wrep_last_use = [0, 0]
            hm_last_use = 0
            dev_evac = [0] * 8

            def load_weights(it):
                e_, half = divmod(it, 2)
                s = it % 2
                p.wait("gpsimd", ten, wlast_use[s])
                if e_ < NE:
                    g_src = bass.AP(d["gate_w"], e_ * CD * IF + half * 1024,
                                    [[IF, P], [P * IF, 8], [1, 1024]])
                    u_src = bass.AP(d["up_w"], e_ * CD * IF + half * 1024,
                                    [[IF, P], [P * IF, 8], [1, 1024]])
                    dn_src = bass.AP(d["down_w"], e_ * IF * CD + half * 1024 * CD,
                                     [[CD, P], [P * CD, 8], [1, 1024]])
                else:
                    g_src = bass.AP(d["sh_gate"], half * 1024,
                                    [[IF, P], [P * IF, 8], [1, 1024]])
                    u_src = bass.AP(d["sh_up"], half * 1024,
                                    [[IF, P], [P * IF, 8], [1, 1024]])
                    dn_src = bass.AP(d["sh_down"], half * 1024 * CD,
                                     [[CD, P], [P * CD, 8], [1, 1024]])
                p.dma("gpsimd", GW[s][:], g_src, sem=gq)
                p.dma("gpsimd", UW[s][:], u_src, sem=gq)
                p.dma("gpsimd", DW[s][:], dn_src, sem=gq)
                wready[s] = gq.n

            load_weights(0)
            p.wait("sync", gq, mbuf_done)

            psd_read = [0, 0]
            hm_consumed = 0
            xe_last_use = ten.n
            nt_ready = [0, 0]
            nt_use_ten = [0, 0]
            nt_use_vec = [0, 0]
            for it in range(18):
                e_, half = divmod(it, 2)
                s = it % 2
                routed = e_ < NE
                if it + 1 < 18:
                    load_weights(it + 1)
                p.wait("tensor", gq, wready[s])
                for tm in range(NTM):
                    t0 = tm * TM
                    nts = tm % 2
                    p.wait("sync", ten, nt_use_ten[nts])
                    p.wait("sync", vec, nt_use_vec[nts])
                    p.dma("sync", NT[:, nts * CH * TM:(nts + 1) * CH * TM],
                          bass.AP(d["nbuf"], t0, [[T, P], [T * P, CH], [1, TM]]),
                          sem=ld)
                    nt_ready[nts] = ld.n

                    def ntile(c, nts=nts):
                        return NT[:, nts * CH * TM + c * TM:
                                  nts * CH * TM + (c + 1) * TM]

                    if routed:
                        w = tm % 2
                        p.wait("sync", vec, wrep_last_use[w])
                        p.dma("sync", WREP[:, w * TM:(w + 1) * TM],
                              bass.AP(d["mbuf"], e_ * T + t0, [[0, P], [1, TM]]),
                              sem=ld)
                        p.wait("vector", ld, nt_ready[nts])
                        p.wait("vector", ld, ld.n)
                        p.wait("vector", ten, xe_last_use)
                        for c in range(CH):
                            p.emit("vector", lambda e, c=c, w=w, ntile=ntile:
                                   e.tensor_tensor(
                                       out=XE[:, c * TM:(c + 1) * TM],
                                       in0=ntile(c),
                                       in1=WREP[:, w * TM:(w + 1) * TM],
                                       op=ALU.mult),
                                   inc=vec, inc_n=(1 if c == CH - 1 else 0))
                        wrep_last_use[w] = vec.n
                        nt_use_vec[nts] = vec.n
                        p.wait("tensor", vec, vec.n)

                        def rhs(c):
                            return XE[:, c * TM:(c + 1) * TM]
                    else:
                        p.wait("tensor", ld, nt_ready[nts])

                        def rhs(c, ntile=ntile):
                            return ntile(c)

                    p.wait("vector", ten, hm_consumed)
                    for ic in range(8):
                        b = ic % 2
                        p.wait("tensor", act, silu_done[b])
                        p.wait("tensor", vec, hm_written[b])
                        for c in range(CH):
                            p.emit("tensor", lambda e, c=c, ic=ic, s=s, b=b,
                                   rhs=rhs: e.matmul(
                                       ps_g[b][:],
                                       GW[s][:, c * 1024 + ic * P:
                                             c * 1024 + (ic + 1) * P],
                                       rhs(c), start=(c == 0),
                                       stop=(c == CH - 1)))
                        for c in range(CH):
                            p.emit("tensor", lambda e, c=c, ic=ic, s=s, b=b,
                                   rhs=rhs: e.matmul(
                                       ps_u[b][:],
                                       UW[s][:, c * 1024 + ic * P:
                                             c * 1024 + (ic + 1) * P],
                                       rhs(c), start=(c == 0),
                                       stop=(c == CH - 1)),
                                   inc=ten, inc_n=(1 if c == CH - 1 else 0))
                        gu_done = ten.n
                        if routed:
                            xe_last_use = ten.n
                        else:
                            nt_use_ten[nts] = ten.n
                        wlast_use[s] = max(wlast_use[s], ten.n)
                        p.wait("scalar", ten, gu_done)
                        p.emit("scalar", lambda e, b=b: e.activation(
                            out=SG[:, b * TM:(b + 1) * TM], in_=ps_g[b][:],
                            func=AF.Silu), inc=act)
                        silu_done[b] = act.n
                        p.wait("vector", act, silu_done[b])
                        p.emit("vector", lambda e, b=b, ic=ic: e.tensor_tensor(
                            out=HM[:, ic * TM:(ic + 1) * TM], in0=ps_u[b][:],
                            in1=SG[:, b * TM:(b + 1) * TM], op=ALU.mult),
                            inc=vec)
                        hm_written[b] = vec.n
                    hm_all = vec.n
                    p.wait("tensor", vec, hm_all)
                    for oc in range(8):
                        b = oc % 2
                        dv = oc % 4
                        p.wait("scalar", acc, acc.n)
                        p.wait("tensor", act, psd_read[b])
                        for ic in range(8):
                            p.emit("tensor", lambda e, ic=ic, oc=oc, s=s, b=b:
                                   e.matmul(ps_d[b][:],
                                            DW[s][:, ic * 1024 + oc * P:
                                                  ic * 1024 + (oc + 1) * P],
                                            HM[:, ic * TM:(ic + 1) * TM],
                                            start=(ic == 0), stop=(ic == 7)),
                                   inc=ten, inc_n=(1 if ic == 7 else 0))
                        hm_consumed = ten.n
                        wlast_use[s] = max(wlast_use[s], ten.n)
                        p.wait("scalar", ten, ten.n)
                        p.emit("scalar", lambda e, dv=dv, b=b: e.copy(
                            out=DEV[:, dv * TM:(dv + 1) * TM], in_=ps_d[b][:]),
                            inc=act)
                        dev_evac[oc] = act.n
                        psd_read[b] = act.n
                        p.wait("gpsimd", act, dev_evac[oc])
                        kw = {} if it == 0 else {"accum_op": ALU.add}
                        p.dma("gpsimd",
                              d["moe_acc"].ap()[oc * P:(oc + 1) * P,
                                                t0:t0 + TM],
                              DEV[:, dv * TM:(dv + 1) * TM], sem=acc, **kw)
            all_acc = acc.n

            p.wait("sync", acc, acc.n)
            p.wait("sync", gq, gq.n)
            for tm in range(NTM):
                t0 = tm * TM
                for h4 in range(2):
                    cs = [h4 * 4 + i for i in range(4)]
                    prev_gq = gq.n
                    p.wait("sync", vec, vec.n)
                    p.wait("sync", gq, prev_gq)
                    for i, c in enumerate(cs):
                        p.dma("sync", DEV[:, i * TM:(i + 1) * TM],
                              d["moe_acc"].ap()[c * P:(c + 1) * P, t0:t0 + TM],
                              sem=ld)
                        p.dma("sync", UT[:, c * TM:(c + 1) * TM],
                              d["u_buf"].ap()[c * P:(c + 1) * P, t0:t0 + TM],
                              sem=ld)
                        p.dma("sync", XT[:, c * TM:(c + 1) * TM],
                              src.ap()[c * P:(c + 1) * P, t0:t0 + TM], sem=ld)
                    p.wait("vector", ld, ld.n)
                    p.wait("vector", gq, prev_gq)
                    for i, c in enumerate(cs):
                        p.emit("vector", lambda e, i=i, c=c: e.tensor_add(
                            out=DEV[:, i * TM:(i + 1) * TM],
                            in0=DEV[:, i * TM:(i + 1) * TM],
                            in1=UT[:, c * TM:(c + 1) * TM]), inc=vec)
                        p.wait("vector", vec, vec.n)
                        p.emit("vector", lambda e, i=i, c=c: e.tensor_add(
                            out=DEV[:, i * TM:(i + 1) * TM],
                            in0=DEV[:, i * TM:(i + 1) * TM],
                            in1=XT[:, c * TM:(c + 1) * TM]), inc=vec)
                    p.wait("gpsimd", vec, vec.n)
                    for i, c in enumerate(cs):
                        p.dma("gpsimd",
                              dst.ap()[c * P:(c + 1) * P, t0:t0 + TM],
                              DEV[:, i * TM:(i + 1) * TM], sem=gq)
            self.barrier()
            self.p.flush()


    def phase_attn(self, src, dst):
        self._phase_prefix()
        nc, p, d = self.nc, self.p, self.dram
        ld, gq, vec, act, ten = self.ld, self.gq, self.vec, self.act, self.ten
        p.wait("sync", self.gq, self.gq.n)
        p.wait("sync", self.acc, self.acc.n)
        p.wait("gpsimd", self.ld, self.ld.n)
        with ExitStack() as ctx:
            E = ctx.enter_context
            XT = E(nc.sbuf_tensor(self.u + "a_xt", [P, CH * TM], F32))
            UT = E(nc.sbuf_tensor(self.u + "a_ut", [P, CH * TM], F32))
            XLBT = E(nc.sbuf_tensor(self.u + "a_xlbt", [P, CH * TM], BF16))
            ST = E(nc.sbuf_tensor(self.u + "a_st", [1, 4 * TM], F32))
            MUR = E(nc.sbuf_tensor(self.u + "a_mur", [P, TM], F32))
            RSR = E(nc.sbuf_tensor(self.u + "a_rsr", [P, TM], F32))
            QKVE = E(nc.sbuf_tensor(self.u + "a_qkve", [P, 2 * TM], BF16))
            QH = E(nc.sbuf_tensor(self.u + "a_qh", [64, NSEQ], BF16))
            KH = E(nc.sbuf_tensor(self.u + "a_kh", [64, NSEQ], BF16))
            VH = E(nc.sbuf_tensor(self.u + "a_vh", [64, NSEQ], BF16))
            OFU = E(nc.sbuf_tensor(self.u + "a_ofu", [P, CH * NSEQ], BF16))
            QTA = E(nc.sbuf_tensor(self.u + "a_qta", [66, NSEQ], BF16))
            KTA = E(nc.sbuf_tensor(self.u + "a_kta", [66, NSEQ], BF16))
            Q2 = E(nc.sbuf_tensor(self.u + "a_q2", [2, NSEQ], BF16))
            K2 = E(nc.sbuf_tensor(self.u + "a_k2", [2, NSEQ], BF16))
            RPK = E(nc.sbuf_tensor(self.u + "a_rpk", [64, NBK], BF16))
            RPQ = E(nc.sbuf_tensor(self.u + "a_rpq", [64, NBK], BF16))
            RPKD = E(nc.sbuf_tensor(self.u + "a_rpkd", [64, NBK], BF16))
            RPQD = E(nc.sbuf_tensor(self.u + "a_rpqd", [64, NBK], BF16))
            RPK0 = E(nc.sbuf_tensor(self.u + "a_rpk0", [64, 2], BF16))
            RPQ0 = E(nc.sbuf_tensor(self.u + "a_rpq0", [64, 2], BF16))
            RPVT = E(nc.sbuf_tensor(self.u + "a_rpvt", [64, NBK], F32))
            BANDW = E(nc.sbuf_tensor(self.u + "a_bandw", [NBK, 64], BF16))
            FIXB = E(nc.sbuf_tensor(self.u + "a_fixb", [64, 2], BF16))
            FIXR = E(nc.sbuf_tensor(self.u + "a_fixr", [2, 64], BF16))
            VT = E(nc.sbuf_tensor(self.u + "a_vt", [P, 16 * 64], BF16))
            DQTM = [E(nc.sbuf_tensor(self.u + "a_dqtm%d" % i, [P, NBK], BF16))
                    for i in range(2)]
            FQ = [E(nc.sbuf_tensor(self.u + "a_fq%d" % i, [P, 512], BF16))
                  for i in range(2)]
            PQA = E(nc.sbuf_tensor(self.u + "a_pqa", [P, 16 * 384], BF16))
            DKT = E(nc.sbuf_tensor(self.u + "a_dkt", [P, 48 * P], BF16))
            WK = [E(nc.sbuf_tensor(self.u + "a_wk%d" % i, [P, 384], BF16))
                  for i in range(2)]
            EB = E(nc.sbuf_tensor(self.u + "a_eb", [P, 2 * NSEQ], BF16))
            NMX4 = E(nc.sbuf_tensor(self.u + "a_nmx4", [P, 4], F32))
            ETS = E(nc.sbuf_tensor(self.u + "a_ets", [P, 16 * 512], BF16))
            BEXT = E(nc.sbuf_tensor(self.u + "a_bext", [P, 64], BF16))
            BEXTT = E(nc.sbuf_tensor(self.u + "a_bextt", [NBK, 4 * P], BF16))
            MSCR = E(nc.sbuf_tensor(self.u + "a_mscr", [P, 384], BF16))
            MSK3 = E(nc.sbuf_tensor(self.u + "a_msk3", [P, 3 * P], F32))
            ZTILE = E(nc.sbuf_tensor(self.u + "a_zt", [P, P], BF16))
            ONEROW = E(nc.sbuf_tensor(self.u + "a_onerow", [1, 512], BF16))
            AUGK = E(nc.sbuf_tensor(self.u + "a_augk", [1, 512], BF16))
            AUGK2 = E(nc.sbuf_tensor(self.u + "a_augk2", [1, 512], BF16))
            RROWS = E(nc.sbuf_tensor(self.u + "a_rrows", [1, P], F32))
            NMAX = E(nc.sbuf_tensor(self.u + "a_nmax", [P, 1], F32))
            A1 = E(nc.sbuf_tensor(self.u + "a_a1", [P, 1], F32))
            A2 = E(nc.sbuf_tensor(self.u + "a_a2", [P, 1], F32))
            SR = E(nc.sbuf_tensor(self.u + "a_sr", [P, 1], F32))
            RS = E(nc.sbuf_tensor(self.u + "a_rs", [P, 1], F32))
            RB = E(nc.sbuf_tensor(self.u + "a_rb", [P, 1], F32))
            RINV = E(nc.sbuf_tensor(self.u + "a_rinv", [P, 1], F32))
            RREP = E(nc.sbuf_tensor(self.u + "a_rrep", [P, NSEQ], F32))
            XR = E(nc.sbuf_tensor(self.u + "a_xr", [P, 2 * TM], F32))
            OUTT = E(nc.sbuf_tensor(self.u + "a_outt", [P, 2 * TM], F32))
            BPJ = E(nc.sbuf_tensor(self.u + "a_bpj", [P, CH], F32))
            WPB = [E(nc.sbuf_tensor(self.u + "a_wpb%d" % i, [P, CH * P], BF16))
                   for i in range(2)]
            ps_s = E(nc.psum_tensor(self.u + "a_pss", [P, NSEQ], F32))
            ps_o = E(nc.psum_tensor(self.u + "a_pso", [64, 512], F32))
            ps_et = [E(nc.psum_tensor(self.u + "a_pset%d" % i, [P, 1024], BF16))
                     for i in range(2)]
            ps_sm = E(nc.psum_tensor(self.u + "a_psm", [P, 512], F32))

            p.dma("sync", MSK3[:], d["mask3"].ap(), sem=ld)
            p.dma("sync", BPJ[:], bass.AP(d["b_proj"], 0, [[1, P], [P, CH]]),
                  sem=ld, allow_slow_non_contiguous=True)
            p.wait("vector", ld, ld.n)
            p.emit("vector", lambda e: e.memset(ZTILE[:], 0.0))
            p.emit("vector", lambda e: e.memset(ONEROW[:], 1.0))
            p.emit("vector", lambda e: e.memset(K2[0:1, :], 1.0), inc=vec)
            zt_ready = vec.n
            p.wait("gpsimd", vec, zt_ready)
            for zoff in (0, 128, 256, 384, 512):
                p.dma("gpsimd", bass.AP(d["ebuf"], zoff, [[640, P], [1, P]]),
                      ZTILE[:], sem=gq)
            for s4 in range(4):
                p.dma("gpsimd", QTA[65:66, s4 * 512:(s4 + 1) * 512],
                      ONEROW[:], sem=gq)
                p.dma("gpsimd", KTA[64:65, s4 * 512:(s4 + 1) * 512],
                      ONEROW[:], sem=gq)
                p.dma("gpsimd", Q2[1:2, s4 * 512:(s4 + 1) * 512],
                      ONEROW[:], sem=gq)

            wq_ready = [0, 0]
            wq_use = [0, 0]

            def load_wqkv(o, sl):
                p.wait("gpsimd", ten, wq_use[sl])
                p.dma("gpsimd", WPB[sl][:],
                      bass.AP(d["wqkv_t"], o * P,
                              [[3 * CD, P], [P * 3 * CD, CH], [1, P]]),
                      sem=gq)
                wq_ready[sl] = gq.n

            def load_wproj(o, sl):
                p.wait("gpsimd", ten, wq_use[sl])
                p.dma("gpsimd", WPB[sl][:],
                      bass.AP(d["wproj_t"], o * P,
                              [[CD, P], [P * CD, CH], [1, P]]),
                      sem=gq)
                wq_ready[sl] = gq.n

            for b in range(NB):
                tb0 = b * NSEQ
                qkv_st = gq.n
                for tm in range(NSEQ // TM):
                    t0 = tb0 + tm * TM
                    prev_vec, prev_ten, prev_act = vec.n, ten.n, act.n
                    p.wait("sync", vec, prev_vec)
                    p.wait("sync", ten, prev_ten)
                    p.wait("sync", act, prev_act)
                    for c in range(CH):
                        p.dma("sync", XT[:, c * TM:(c + 1) * TM],
                              src.ap()[c * P:(c + 1) * P, t0:t0 + TM], sem=ld)
                    p.wait("vector", ld, ld.n)
                    srcs = [XT[:, c * TM:(c + 1) * TM] for c in range(CH)]
                    self.ln_stats(srcs, TM, UT, ps_o[0:1, 0:TM],
                                  ps_sm[0:1, 0:TM], ST, MUR, RSR)
                    p.wait("vector", ten, ten.n)
                    for c in range(CH):
                        p.emit("vector", lambda e, c=c: e.tensor_sub(
                            out=UT[:, c * TM:(c + 1) * TM], in0=srcs[c],
                            in1=MUR[:]), inc=vec)
                        p.wait("vector", vec, vec.n)
                        p.emit("vector", lambda e, c=c: e.tensor_tensor(
                            out=XLBT[:, c * TM:(c + 1) * TM],
                            in0=UT[:, c * TM:(c + 1) * TM],
                            in1=RSR[:], op=ALU.mult), inc=vec)
                    xlb_done = vec.n
                    p.wait("tensor", vec, xlb_done)
                    ev_last = act.n
                    qkv_ev = {0: act.n, 1: act.n}
                    for o in range(24):
                        sl = o % 2
                        qsl = o % 2
                        if b == 0 and tm == 0 and o == 0:
                            load_wqkv(0, 0)
                        if o + 1 < 24:
                            load_wqkv(o + 1, (o + 1) % 2)
                        elif tm + 1 < NSEQ // TM or b == 0:
                            load_wqkv(0, (o + 1) % 2)
                        pq = ps_s[:, qsl * 512:(qsl + 1) * 512]
                        p.wait("tensor", gq, wq_ready[sl])
                        p.wait("tensor", act, qkv_ev[qsl])
                        for c in range(CH):
                            p.emit("tensor", lambda e, c=c, sl=sl, pq=pq:
                                   e.matmul(
                                       pq, WPB[sl][:, c * P:(c + 1) * P],
                                       XLBT[:, c * TM:(c + 1) * TM],
                                       start=(c == 0), stop=(c == CH - 1)),
                                   inc=ten, inc_n=(1 if c == CH - 1 else 0))
                        wq_use[sl] = ten.n
                        p.wait("scalar", ten, ten.n)
                        p.wait("scalar", gq, qkve_st if (b + tm + o) else 0)
                        if o < 8:
                            p.emit("scalar", lambda e, qsl=qsl, pq=pq:
                                   e.activation(
                                       out=QKVE[:, qsl * TM:(qsl + 1) * TM],
                                       in_=pq, func=AF.Identity, scale=SCALE),
                                   inc=act)
                        else:
                            p.emit("scalar", lambda e, qsl=qsl, pq=pq: e.copy(
                                out=QKVE[:, qsl * TM:(qsl + 1) * TM],
                                in_=pq), inc=act)
                        qkv_ev[qsl] = act.n
                        p.wait("gpsimd", act, act.n)
                        p.dma("gpsimd",
                              bass.AP(d["qkvbuf"], o * P * NSEQ + tm * TM,
                                      [[NSEQ, P], [1, TM]]),
                              QKVE[:, qsl * TM:(qsl + 1) * TM], sem=gq)
                        qkve_st = gq.n
                qkv_written = gq.n

                of_scaled = vec.n
                augk_dma = 0
                for h in range(ATTN_LIMIT["heads"]):
                    hc, hb = h // 2, (h % 2) * 64
                    p.wait("sync", gq, qkv_written)
                    p.wait("sync", ten, ten.n)
                    p.wait("sync", vec, vec.n)
                    p.wait("sync", act, act.n)
                    p.dma("sync", QH[:],
                          bass.AP(d["qkvbuf"], h * 64 * NSEQ,
                                  [[NSEQ, 64], [1, NSEQ]]), sem=ld)
                    p.dma("sync", KH[:],
                          bass.AP(d["qkvbuf"], (CD + h * 64) * NSEQ,
                                  [[NSEQ, 64], [1, NSEQ]]), sem=ld)
                    p.dma("sync", VH[:],
                          bass.AP(d["qkvbuf"], (2 * CD + h * 64) * NSEQ,
                                  [[NSEQ, 64], [1, NSEQ]]), sem=ld)
                    qkvh_ld = ld.n
                    p.wait("gpsimd", vec, vec.n)
                    p.wait("gpsimd", ten, ten.n)
                    p.dma("gpsimd", RPK[:],
                          bass.AP(d["rpe_k"], h * HD * NBK,
                                  [[NBK, 64], [1, NBK]]), sem=gq)
                    p.dma("gpsimd", RPQ[:],
                          bass.AP(d["rpe_q"], h * HD * NBK,
                                  [[NBK, 64], [1, NBK]]), sem=gq)
                    p.dma("gpsimd", BANDW[:],
                          bass.AP(d["rpe_v"], h * NBK * HD,
                                  [[HD, NBK], [1, HD]]), sem=gq)
                    p.dma("gpsimd", RPVT[:],
                          bass.AP(d["rpe_v_t"], h * HD * NBK,
                                  [[NBK, 64], [1, NBK]]), sem=gq)
                    rpe_ld = gq.n
                    p.wait("vector", gq, rpe_ld)
                    p.emit("vector", lambda e: e.tensor_sub(
                        out=RPKD[:], in0=RPK[:],
                        in1=RPK[:, 0:1].broadcast_to((64, NBK))))
                    p.emit("vector", lambda e: e.tensor_sub(
                        out=RPQD[:], in0=RPQ[:],
                        in1=RPQ[:, 62:63].broadcast_to((64, NBK))))
                    p.emit("vector", lambda e: e.tensor_copy(
                        out=RPK0[:, 0:1], in_=RPK[:, 0:1]))
                    p.emit("vector", lambda e: e.tensor_sub(
                        out=RPK0[:, 1:2], in0=RPK[:, 62:63], in1=RPK[:, 0:1]))
                    p.emit("vector", lambda e: e.tensor_copy(
                        out=RPQ0[:, 0:1], in_=RPQ[:, 62:63]))
                    p.emit("vector", lambda e: e.tensor_sub(
                        out=RPQ0[:, 1:2], in0=RPQ[:, 0:1], in1=RPQ[:, 62:63]))
                    p.emit("vector", lambda e: e.tensor_scalar_mul(
                        out=FIXB[:, 0:1], in0=RPVT[:, 0:1], scalar1=-1.0))
                    p.emit("vector", lambda e: e.tensor_sub(
                        out=FIXB[:, 1:2], in0=RPVT[:, 62:63],
                        in1=RPVT[:, 0:1]), inc=vec)
                    rpe_cols = vec.n

                    p.wait("gpsimd", ld, qkvh_ld)
                    p.dma("gpsimd", QTA[0:64, :], QH[:], sem=gq)
                    p.dma("gpsimd", KTA[0:64, :], KH[:], sem=gq)
                    qta_ld = gq.n

                    if not ATTN_LIMIT["prep"]:
                        of_scaled = vec.n
                        continue
                    p.wait("tensor", gq, qta_ld)
                    p.wait("tensor", vec, rpe_cols)
                    p.wait("tensor", act, act.n)
                    p.wait("tensor", vec, of_scaled)
                    for tm in range(4 if ATTN_LIMIT["aug"] else 0):
                        s0, s1 = tm * 512, (tm + 1) * 512
                        p.wait("tensor", act, act.n)
                        p.emit("tensor", lambda e, s0=s0, s1=s1: e.matmul(
                            ps_s[0:1, 0:512], RPK0[:, 0:1],
                            QTA[0:64, s0:s1], start=True, stop=True))
                        p.emit("tensor", lambda e, s0=s0, s1=s1: e.matmul(
                            ps_s[0:1, 512:1024], RPK0[:, 1:2],
                            QTA[0:64, s0:s1], start=True, stop=True))
                        p.emit("tensor", lambda e, s0=s0, s1=s1: e.matmul(
                            ps_s[0:1, 1024:1536], RPQ0[:, 0:1],
                            KTA[0:64, s0:s1], start=True, stop=True))
                        p.emit("tensor", lambda e, s0=s0, s1=s1: e.matmul(
                            ps_s[0:1, 1536:2048], RPQ0[:, 1:2],
                            KTA[0:64, s0:s1], start=True, stop=True),
                            inc=ten)
                        p.wait("scalar", ten, ten.n)
                        p.wait("scalar", gq, augk_dma)
                        p.emit("scalar", lambda e, s0=s0, s1=s1: e.copy(
                            out=QTA[64:65, s0:s1], in_=ps_s[0:1, 0:512]))
                        p.emit("scalar", lambda e, s0=s0, s1=s1: e.copy(
                            out=Q2[0:1, s0:s1], in_=ps_s[0:1, 512:1024]))
                        p.emit("scalar", lambda e: e.activation(
                            out=AUGK[:], in_=ps_s[0:1, 1024:1536],
                            func=AF.Identity, scale=SCALE))
                        p.emit("scalar", lambda e: e.activation(
                            out=AUGK2[:], in_=ps_s[0:1, 1536:2048],
                            func=AF.Identity, scale=SCALE), inc=act)
                        p.wait("gpsimd", act, act.n)
                        p.dma("gpsimd", KTA[65:66, s0:s1], AUGK[:], sem=gq)
                        p.dma("gpsimd", K2[1:2, s0:s1], AUGK2[:], sem=gq)
                        augk_dma = gq.n
                    aug_dma = gq.n

                    p.emit("tensor", lambda e: e.matmul(
                        ps_sm[0:2, 0:64], FIXB[:],
                        self.sb["ident_bf"][0:64, 0:64], start=True,
                        stop=True), inc=ten)
                    p.wait("scalar", ten, ten.n)
                    p.emit("scalar", lambda e: e.copy(out=FIXR[:],
                                                      in_=ps_sm[0:2, 0:64]),
                           inc=act)
                    p.wait("gpsimd", act, act.n)
                    p.wait("gpsimd", gq, rpe_ld)
                    p.dma("gpsimd", BANDW[0:1, :], FIXR[0:1, :], sem=gq)
                    p.dma("gpsimd", BANDW[62:63, :], FIXR[1:2, :], sem=gq)
                    bandw_ready = gq.n

                    p.wait("vector", ld, qkvh_ld)
                    p.wait("vector", gq, rpe_ld)
                    p.emit("vector", lambda e: e.tensor_scalar_add(
                        out=VH[:], in0=VH[:], scalar1=RPVT[:, 0:1]), inc=vec)
                    vbar_done = vec.n
                    p.wait("tensor", vec, vbar_done)
                    vt_ev = {0: act.n, 1: act.n}
                    for jt in range(16 if ATTN_LIMIT["vt"] else 0):
                        sl = jt % 2
                        p.wait("tensor", act, vt_ev[sl])
                        p.emit("tensor", lambda e, jt=jt, sl=sl: e.transpose(
                            ps_et[sl][:, 0:64], VH[:, jt * P:(jt + 1) * P],
                            self.sb["ident_bf"][0:64, 0:64]), inc=ten)
                        p.wait("scalar", ten, ten.n)
                        p.emit("scalar", lambda e, jt=jt, sl=sl: e.copy(
                            out=VT[:, jt * 64:(jt + 1) * 64],
                            in_=ps_et[sl][:, 0:64]), inc=act)
                        vt_ev[sl] = act.n
                    vt_done = act.n

                    pw_mm = {}
                    pw_ev = {0: act.n, 1: act.n}
                    pw_vec = {0: 0, 1: 0}
                    pw_dma = {0: 0, 1: 0}
                    for qb in range(16 if ATTN_LIMIT["dq"] else 0):
                        bk = qb % 4
                        sl = qb % 2
                        p.wait("tensor", act, pw_mm.get(bk, 0))
                        p.emit("tensor", lambda e, qb=qb, bk=bk: e.matmul(
                            ps_s[:, bk * 512:bk * 512 + NBK],
                            QTA[0:64, qb * P:(qb + 1) * P],
                            RPKD[:], start=True, stop=True), inc=ten)
                        p.wait("scalar", ten, ten.n)
                        p.wait("scalar", vec, pw_vec[sl])
                        p.emit("scalar", lambda e, sl=sl, bk=bk: e.copy(
                            out=DQTM[sl][:],
                            in_=ps_s[:, bk * 512:bk * 512 + NBK]), inc=act)
                        pw_mm[bk] = act.n
                        pw_ev[sl] = act.n
                        p.wait("vector", act, pw_ev[sl])
                        p.wait("vector", gq, pw_dma[sl])
                        p.emit("vector", lambda e, sl=sl: e.memset(
                            FQ[sl][:, 0:225], 0.0))
                        p.emit("vector", lambda e, sl=sl: e.tensor_copy(
                            out=FQ[sl][:, 225:286], in_=DQTM[sl][:, 1:62]))
                        p.emit("vector", lambda e, sl=sl: e.tensor_copy(
                            out=FQ[sl][:, 286:512],
                            in_=DQTM[sl][:, 62:63].broadcast_to((P, 226))),
                            inc=vec)
                        pw_vec[sl] = vec.n
                        p.wait("gpsimd", vec, pw_vec[sl])
                        p.dma("gpsimd",
                              bass.AP(d["pqbuf"], qb * P * 512,
                                      [[512, P], [1, 512]]), FQ[sl][:], sem=gq)
                        pw_dma[sl] = gq.n
                    pq_written = gq.n
                    p.wait("gpsimd", gq, pq_written)
                    p.wait("gpsimd", ten, ten.n)
                    for qb in range(16 if ATTN_LIMIT["dq"] else 0):
                        p.dma("gpsimd", PQA[:, qb * 384:(qb + 1) * 384],
                              bass.AP(d["pqbuf"], qb * P * 512 + 127,
                                      [[511, P], [1, 384]]), sem=gq)
                    pqa_ready = gq.n

                    pw_mm = {}
                    pw_ev = {0: act.n, 1: act.n}
                    pw_vec = {0: 0, 1: 0}
                    pw_dma = {0: 0, 1: 0}
                    for jb in range(16 if ATTN_LIMIT["dk"] else 0):
                        bk = jb % 4
                        sl = jb % 2
                        p.wait("tensor", act, pw_mm.get(bk, 0))
                        p.emit("tensor", lambda e, jb=jb, bk=bk: e.matmul(
                            ps_s[:, bk * 512:bk * 512 + NBK],
                            KTA[0:64, jb * P:(jb + 1) * P],
                            RPQD[:], start=True, stop=True), inc=ten)
                        p.wait("scalar", ten, ten.n)
                        p.wait("scalar", vec, pw_vec[sl])
                        p.emit("scalar", lambda e, sl=sl, bk=bk: e.activation(
                            out=DQTM[sl][:],
                            in_=ps_s[:, bk * 512:bk * 512 + NBK],
                            func=AF.Identity, scale=SCALE), inc=act)
                        pw_mm[bk] = act.n
                        pw_ev[sl] = act.n
                        p.wait("vector", act, pw_ev[sl])
                        p.wait("vector", gq, pw_dma[sl])
                        p.emit("vector", lambda e, sl=sl: e.tensor_copy(
                            out=FQ[sl][:, 0:225],
                            in_=DQTM[sl][:, 0:1].broadcast_to((P, 225))))
                        p.emit("vector", lambda e, sl=sl: e.tensor_copy(
                            out=FQ[sl][:, 225:286], in_=DQTM[sl][:, 1:62]))
                        p.emit("vector", lambda e, sl=sl: e.memset(
                            FQ[sl][:, 286:512], 0.0), inc=vec)
                        pw_vec[sl] = vec.n
                        p.wait("gpsimd", vec, pw_vec[sl])
                        p.dma("gpsimd",
                              bass.AP(d["pkbuf"], jb * P * 512,
                                      [[512, P], [1, 512]]), FQ[sl][:], sem=gq)
                        pw_dma[sl] = gq.n
                    pk_written = gq.n
                    dkt_ev = {0: act.n, 1: act.n}
                    wk_use = {0: 0, 1: 0}
                    p.wait("sync", gq, pk_written)
                    for jb in range(16 if ATTN_LIMIT["dk"] else 0):
                        wsl = jb % 2
                        p.wait("sync", ten, wk_use[wsl])
                        p.dma("sync", WK[wsl][:],
                              bass.AP(d["pkbuf"], jb * P * 512 + 127,
                                      [[511, P], [1, 384]]), sem=ld)
                        wk_ld = ld.n
                        p.wait("tensor", ld, wk_ld)
                        for s3 in range(3):
                            qb = jb - 1 + s3
                            if qb < 0 or qb > 15:
                                continue
                            sl = s3 % 2
                            pair = qb * 3 + (jb - qb + 1)
                            p.wait("tensor", act, dkt_ev[sl])
                            p.emit("tensor", lambda e, s3=s3, sl=sl, wsl=wsl:
                                   e.transpose(
                                       ps_et[sl][:, 0:P],
                                       WK[wsl][:, s3 * P:(s3 + 1) * P],
                                       self.sb["ident_bf"][:]), inc=ten)
                            wk_use[wsl] = ten.n
                            p.wait("scalar", ten, ten.n)
                            p.emit("scalar", lambda e, pair=pair, sl=sl:
                                   e.copy(out=DKT[:, pair * P:(pair + 1) * P],
                                          in_=ps_et[sl][:, 0:P]), inc=act)
                            dkt_ev[sl] = act.n
                    dkt_done = act.n

                    if not ATTN_LIMIT["score"]:
                        of_scaled = vec.n
                        continue
                    p.wait("tensor", gq, aug_dma)
                    p.wait("tensor", act, dkt_done)
                    p.wait("tensor", gq, pqa_ready)
                    p.wait("tensor", gq, bandw_ready)
                    p.wait("tensor", act, vt_done)
                    ebs_guard = [
                        {"ten": 0, "vec": 0, "gq": 0, "act": 0},
                        {"ten": 0, "vec": 0, "gq": 0, "act": 0}]
                    bx_used = 0
                    eb_band_ld = 0
                    pend_et = None
                    for qm in range(4):
                        for qbl in range(4):
                            qb = qm * 4 + qbl
                            q0 = qb * P
                            ebsl = qb % 2
                            eb0 = ebsl * NSEQ
                            p.wait("tensor", act, act.n)
                            p.wait("tensor", vec, vec.n)
                            for jm in range(4):
                                j0m = jm * 512
                                r2lo = max(q0 + 256, j0m)
                                r2hi = min(2048, j0m + 512)
                                plo = max(q0 - 128, 0, j0m)
                                phi = min(q0 + 256, 2048, j0m + 512)
                                dkops = [jb2 for jb2 in (qb - 1, qb, qb + 1)
                                         if 0 <= jb2 <= 15
                                         and j0m <= jb2 * P < j0m + 512]
                                nops = (1 + (1 if r2hi > r2lo else 0)
                                        + (1 if phi > plo else 0) + len(dkops))
                                k = 1
                                p.emit("tensor", lambda e, q0=q0, jm=jm,
                                       last=(k == nops): e.matmul(
                                           ps_s[:, jm * 512:(jm + 1) * 512],
                                           QTA[:, q0:q0 + P],
                                           KTA[:, jm * 512:(jm + 1) * 512],
                                           start=True, stop=last),
                                       inc=ten, inc_n=(1 if k == nops else 0))
                                if r2hi > r2lo:
                                    k += 1
                                    p.emit("tensor", lambda e, q0=q0,
                                           r2lo=r2lo, r2hi=r2hi,
                                           last=(k == nops): e.matmul(
                                               ps_s[:, r2lo:r2hi],
                                               Q2[:, q0:q0 + P],
                                               K2[:, r2lo:r2hi],
                                               start=False, stop=last),
                                           inc=ten,
                                           inc_n=(1 if k == nops else 0))
                                if phi > plo:
                                    k += 1
                                    off = plo - (q0 - 128)
                                    p.emit("tensor", lambda e, qb=qb, plo=plo,
                                           phi=phi, off=off,
                                           last=(k == nops): e.matmul(
                                               ps_s[:, plo:phi],
                                               self.sb["ident_bf"][:],
                                               PQA[:, qb * 384 + off:
                                                   qb * 384 + off + phi - plo],
                                               start=False, stop=last),
                                           inc=ten,
                                           inc_n=(1 if k == nops else 0))
                                for jb2 in dkops:
                                    k += 1
                                    pair = qb * 3 + (jb2 - qb + 1)
                                    p.emit("tensor", lambda e, jb2=jb2,
                                           pair=pair, last=(k == nops):
                                           e.matmul(
                                               ps_s[:, jb2 * P:(jb2 + 1) * P],
                                               self.sb["ident_bf"][:],
                                               DKT[:, pair * P:(pair + 1) * P],
                                               start=False, stop=last),
                                           inc=ten,
                                           inc_n=(1 if k == nops else 0))
                                bank_done = ten.n
                                p.wait("vector", ten, bank_done)
                                p.emit("vector", lambda e, jm=jm: e.tensor_reduce(
                                    out=NMX4[:, jm:jm + 1],
                                    in_=ps_s[:, jm * 512:(jm + 1) * 512],
                                    axis=AX.X, op=ALU.max),
                                    inc=vec, inc_n=(1 if jm == 3 else 0))
                            qk_done = ten.n
                            if pend_et is not None:
                                pend_et()
                                pend_et = None
                            p.wait("vector", vec, vec.n)
                            p.emit("vector", lambda e: e.tensor_reduce(
                                out=NMAX[:], in_=NMX4[:], axis=AX.X,
                                op=ALU.max, negate=True), inc=vec)
                            p.wait("vector", vec, vec.n)
                            nmax_done = vec.n
                            jsplit = min(q0 + 256, 2048)
                            p.wait("scalar", vec, nmax_done)
                            g_ = ebs_guard[ebsl]
                            p.wait("scalar", vec, g_["vec"])
                            p.wait("scalar", ten, g_["ten"])
                            p.wait("scalar", gq, g_["gq"])
                            p.wait("scalar", act, g_["act"])
                            p.emit("scalar", lambda e, jsplit=jsplit, eb0=eb0:
                                   e.activation(
                                       out=EB[:, eb0:eb0 + jsplit],
                                       in_=ps_s[:, 0:jsplit], func=AF.Exp,
                                       bias=NMAX[:], accum_out=A1[:]),
                                   inc=act)
                            if jsplit < 2048:
                                p.emit("scalar", lambda e, jsplit=jsplit,
                                       eb0=eb0: e.activation(
                                           out=EB[:, eb0 + jsplit:eb0 + 2048],
                                           in_=ps_s[:, jsplit:2048],
                                           func=AF.Exp, bias=NMAX[:],
                                           accum_out=A2[:]), inc=act)
                            else:
                                p.wait("vector", act, act.n)
                                p.emit("vector", lambda e: e.memset(
                                    A2[:], 0.0), inc=vec)
                            exp_done = act.n
                            p.wait("vector", act, exp_done)
                            slo = max(q0 - 128, 0)
                            shi = min(q0 + 256, 2048)
                            moff = slo - (q0 - 128)
                            p.emit("vector", lambda e, slo=slo, shi=shi,
                                   moff=moff, eb0=eb0:
                                   e.scalar_tensor_tensor(
                                       out=MSCR[:, 0:shi - slo],
                                       in0=EB[:, eb0 + slo:eb0 + shi],
                                       scalar=1.0,
                                       in1=MSK3[:, moff:moff + shi - slo],
                                       op0=ALU.mult, op1=ALU.mult,
                                       accum_out=SR[:]), inc=vec)
                            p.wait("vector", vec, vec.n)
                            p.emit("vector", lambda e: e.tensor_add(
                                out=RS[:], in0=A1[:], in1=A2[:]), inc=vec)
                            p.wait("vector", vec, vec.n)
                            p.emit("vector", lambda e: e.tensor_add(
                                out=A2[:], in0=A2[:], in1=SR[:]))
                            p.emit("vector", lambda e: e.reciprocal(
                                out=RINV[:], in_=RS[:]), inc=vec)
                            stats_done = vec.n
                            p.wait("tensor", vec, stats_done)
                            p.wait("tensor", act, act.n)
                            p.emit("tensor", lambda e: e.matmul(
                                ps_sm[0:1, 0:P], RINV[:],
                                self.sb["ident_f"][:], start=True, stop=True),
                                inc=ten)
                            p.wait("scalar", ten, ten.n)
                            p.wait("scalar", ld, rrow_ld if (qb or h or b) else 0)
                            p.emit("scalar", lambda e: e.copy(
                                out=RROWS[:], in_=ps_sm[0:1, 0:P]), inc=act)
                            p.wait("sync", act, act.n)
                            p.dma("sync",
                                  bass.AP(d["rbuf"], q0, [[1, 1], [1, P]]),
                                  RROWS[:], sem=ld)
                            rrow_ld = ld.n
                            p.wait("gpsimd", act, exp_done)
                            p.wait("gpsimd", gq, eb_band_ld)
                            if qb == 0:
                                p.dma("gpsimd",
                                      bass.AP(d["ebuf"], 128,
                                              [[640, P], [1, P]]),
                                      ZTILE[:], sem=gq)
                                p.wait("gpsimd", gq, gq.n)
                            if qb == 15:
                                p.dma("gpsimd",
                                      bass.AP(d["ebuf"], 384,
                                              [[640, P], [1, P]]),
                                      ZTILE[:], sem=gq)
                                p.wait("gpsimd", gq, gq.n)
                            p.dma("gpsimd",
                                  bass.AP(d["ebuf"], 128 + moff,
                                          [[640, P], [1, shi - slo]]),
                                  EB[:, eb0 + slo:eb0 + shi], sem=gq)
                            p.wait("gpsimd", gq, gq.n)
                            p.wait("gpsimd", ten, bx_used)
                            p.wait("gpsimd", vec, vec.n)
                            p.dma("gpsimd", BEXT[:, 1:62],
                                  bass.AP(d["ebuf"], 226,
                                          [[641, P], [1, 61]]),
                                  sem=gq)
                            eb_band_ld = gq.n
                            p.wait("vector", gq, eb_band_ld)
                            p.wait("vector", act, act.n)
                            p.emit("vector", lambda e: e.tensor_reduce(
                                out=RB[:], in_=BEXT[:, 1:62],
                                axis=AX.X, op=ALU.add), inc=vec)
                            p.wait("vector", vec, vec.n)
                            p.emit("vector", lambda e: e.tensor_copy(
                                out=BEXT[:, 0:1], in_=RB[:]))
                            p.emit("vector", lambda e: e.tensor_copy(
                                out=BEXT[:, 62:63], in_=A2[:]), inc=vec)
                            bx_done = vec.n
                            p.wait("tensor", vec, bx_done)
                            p.wait("tensor", act, act.n)
                            p.emit("tensor", lambda e: e.transpose(
                                ps_et[0][0:64, 0:P], BEXT[:, 0:64],
                                self.sb["ident_bf"][:]), inc=ten)
                            bx_used = ten.n
                            p.wait("scalar", ten, ten.n)
                            p.emit("scalar", lambda e, qbl=qbl: e.copy(
                                out=BEXTT[:, qbl * P:(qbl + 1) * P],
                                in_=ps_et[0][0:NBK, 0:P]), inc=act)

                            def make_et(qbl=qbl, ebsl=ebsl, eb0=eb0,
                                        exp_done=exp_done):
                                def go():
                                    et_ev = {0: act.n, 1: act.n}
                                    p.wait("tensor", act, exp_done)
                                    for jt in range(16):
                                        sl = jt % 2
                                        p.wait("tensor", act, et_ev[sl])
                                        p.emit("tensor", lambda e, jt=jt,
                                               sl=sl, eb0=eb0: e.transpose(
                                                   ps_et[sl][:, 0:P],
                                                   EB[:, eb0 + jt * P:
                                                      eb0 + (jt + 1) * P],
                                                   self.sb["ident_bf"][:]),
                                               inc=ten)
                                        ebs_guard[ebsl]["ten"] = ten.n
                                        p.wait("scalar", ten, ten.n)
                                        p.emit("scalar", lambda e, jt=jt,
                                               sl=sl, qbl=qbl: e.copy(
                                                   out=ETS[:, jt * 512
                                                           + qbl * P:
                                                           jt * 512
                                                           + (qbl + 1) * P],
                                                   in_=ps_et[sl][:, 0:P]),
                                               inc=act)
                                        et_ev[sl] = act.n
                                    ebs_guard[ebsl]["vec"] = vec.n
                                    ebs_guard[ebsl]["gq"] = gq.n
                                    ebs_guard[ebsl]["act"] = act.n
                                return go
                            pend_et = make_et()
                        if pend_et is not None:
                            pend_et()
                            pend_et = None
                        ets_done = act.n
                        if not ATTN_LIMIT["av"]:
                            ofu_done = act.n
                            continue
                        p.wait("tensor", act, ets_done)
                        for jt in range(16):
                            p.emit("tensor", lambda e, jt=jt,
                                   first=(jt == 0): e.matmul(
                                       ps_o[:], VT[:, jt * 64:(jt + 1) * 64],
                                       ETS[:, jt * 512:(jt + 1) * 512],
                                       start=first, stop=False))
                        for qbl in range(4):
                            p.emit("tensor", lambda e, qbl=qbl,
                                   last=(qbl == 3): e.matmul(
                                       ps_o[:, qbl * P:(qbl + 1) * P],
                                       BANDW[:],
                                       BEXTT[:, qbl * P:(qbl + 1) * P],
                                       start=False, stop=last),
                                   inc=ten, inc_n=(1 if qbl == 3 else 0))
                        av_done = ten.n
                        p.wait("scalar", ten, av_done)
                        p.emit("scalar", lambda e, qm=qm, hc=hc, hb=hb:
                               e.copy(out=OFU[hb:hb + 64,
                                              hc * NSEQ + qm * 512:
                                              hc * NSEQ + (qm + 1) * 512],
                                      in_=ps_o[:]), inc=act)
                        ofu_done = act.n
                    p.wait("sync", ld, rrow_ld)
                    p.wait("sync", vec, vec.n)
                    p.dma("sync", RREP[:],
                          bass.AP(d["rbuf"], 0, [[0, P], [1, NSEQ]]), sem=ld)
                    rrow_ld = ld.n
                    p.wait("vector", ld, ld.n)
                    p.wait("vector", act, ofu_done)
                    p.emit("vector", lambda e, hc=hc, hb=hb: e.tensor_tensor(
                        out=OFU[hb:hb + 64, hc * NSEQ:(hc + 1) * NSEQ],
                        in0=OFU[hb:hb + 64, hc * NSEQ:(hc + 1) * NSEQ],
                        in1=RREP[hb:hb + 64, :] if hb else RREP[0:64, :],
                        op=ALU.mult), inc=vec)
                    of_scaled = vec.n

                p.wait("tensor", vec, of_scaled)
                proj_ev = {0: vec.n, 1: vec.n}
                for oc in range(CH):
                    sl = oc % 2
                    if oc == 0:
                        load_wproj(0, 0)
                    if oc + 1 < CH:
                        load_wproj(oc + 1, (oc + 1) % 2)
                    elif b == 0:
                        load_wqkv(0, (oc + 1) % 2)
                    p.wait("tensor", gq, wq_ready[sl])
                    for tm in range(4):
                        rsl = tm % 2
                        pq = ps_s[:, rsl * 512:(rsl + 1) * 512]
                        p.wait("tensor", vec, proj_ev[rsl])
                        for c in range(CH):
                            p.emit("tensor", lambda e, c=c, sl=sl, tm=tm,
                                   pq=pq: e.matmul(
                                       pq, WPB[sl][:, c * P:(c + 1) * P],
                                       OFU[:, c * NSEQ + tm * 512:
                                           c * NSEQ + (tm + 1) * 512],
                                       start=(c == 0), stop=(c == CH - 1)),
                                   inc=ten, inc_n=(1 if c == CH - 1 else 0))
                        wq_use[sl] = ten.n
                        mm_done = ten.n
                        p.wait("sync", vec, vec.n)
                        p.wait("sync", gq, gq.n)
                        p.dma("sync", XR[:, rsl * TM:(rsl + 1) * TM],
                              src.ap()[oc * P:(oc + 1) * P,
                                       tb0 + tm * 512:tb0 + (tm + 1) * 512],
                              sem=ld)
                        p.wait("vector", ld, ld.n)
                        p.wait("vector", ten, mm_done)
                        p.emit("vector", lambda e, oc=oc, rsl=rsl, pq=pq:
                               e.scalar_tensor_tensor(
                                   out=OUTT[:, rsl * TM:(rsl + 1) * TM],
                                   in0=pq,
                                   scalar=BPJ[:, oc:oc + 1],
                                   in1=XR[:, rsl * TM:(rsl + 1) * TM],
                                   op0=ALU.add, op1=ALU.add), inc=vec)
                        proj_ev[rsl] = vec.n
                        p.wait("gpsimd", vec, vec.n)
                        p.dma("gpsimd",
                              dst.ap()[oc * P:(oc + 1) * P,
                                       tb0 + tm * 512:tb0 + (tm + 1) * 512],
                              OUTT[:, rsl * TM:(rsl + 1) * TM], sem=gq)
            self.barrier()
            p.flush()

    def phase_mamba(self, src, dst):
        self._phase_prefix()
        nc, p, d = self.nc, self.p, self.dram
        ld, gq, vec, act, ten = self.ld, self.gq, self.vec, self.act, self.ten
        W = 256
        NCH = NSEQ // W
        XCW = W + 6
        p.wait("sync", self.gq, self.gq.n)
        p.wait("sync", self.acc, self.acc.n)
        p.wait("gpsimd", self.ld, self.ld.n)
        with ExitStack() as ctx:
            E = ctx.enter_context
            XT = E(nc.sbuf_tensor(self.u + "m_xt", [P, CH * W], F32))
            UT = E(nc.sbuf_tensor(self.u + "m_ut", [P, CH * W], F32))
            XLB = E(nc.sbuf_tensor(self.u + "m_xlb", [P, CH * W], BF16))
            ST = E(nc.sbuf_tensor(self.u + "m_st", [1, 4 * W], F32))
            MUR = E(nc.sbuf_tensor(self.u + "m_mur", [P, W], F32))
            RSR = E(nc.sbuf_tensor(self.u + "m_rsr", [P, W], F32))
            XC = E(nc.sbuf_tensor(self.u + "m_xc", [P, DIC * XCW], BF16))
            ZS = E(nc.sbuf_tensor(self.u + "m_zs", [P, DIC * W], BF16))
            XBLD = E(nc.sbuf_tensor(self.u + "m_xbld", [P, DIC * W], BF16))
            DTL = E(nc.sbuf_tensor(self.u + "m_dtl", [64, W], BF16))
            BCF = E(nc.sbuf_tensor(self.u + "m_bcf", [96, W], F32))
            DELTA = E(nc.sbuf_tensor(self.u + "m_delta", [P, DIC * W], F32))
            DU = E(nc.sbuf_tensor(self.u + "m_du", [P, DIC * W], BF16))
            DA = E(nc.sbuf_tensor(self.u + "m_da", [P, DST * W], F32))
            DB = E(nc.sbuf_tensor(self.u + "m_db", [P, DST * W], BF16))
            HS = E(nc.sbuf_tensor(self.u + "m_hs", [P, DST * W], F32))
            BREP = E(nc.sbuf_tensor(self.u + "m_brep", [P, DST * W], BF16))
            CREP = E(nc.sbuf_tensor(self.u + "m_crep", [P, DST * W], BF16))
            YT = E(nc.sbuf_tensor(self.u + "m_yt", [P, W], F32))
            Y3 = E(nc.sbuf_tensor(self.u + "m_y3", [P, DIC * W], BF16))
            SP1 = E(nc.sbuf_tensor(self.u + "m_sp1", [P, W], F32))
            SP2 = E(nc.sbuf_tensor(self.u + "m_sp2", [P, W], F32))
            SP3 = E(nc.sbuf_tensor(self.u + "m_sp3", [P, W], F32))
            RES = E(nc.sbuf_tensor(self.u + "m_res", [P, 2 * W], F32))
            STATE = E(nc.sbuf_tensor(self.u + "m_state", [P, DIC * DST], F32))
            CT = E(nc.sbuf_tensor(self.u + "m_ct", [P, 2 * W], F32))
            WINB = [E(nc.sbuf_tensor(self.u + "m_win%d" % i, [P, CH * P], BF16))
                    for i in range(2)]
            WOUTB = [E(nc.sbuf_tensor(self.u + "m_wout%d" % i, [P, DIC * P], BF16))
                     for i in range(2)]
            XPW = E(nc.sbuf_tensor(self.u + "m_xpw", [P, DIC * 96], BF16))
            DTW = E(nc.sbuf_tensor(self.u + "m_dtw", [64, DI], BF16))
            ASB = E(nc.sbuf_tensor(self.u + "m_asb", [P, DIC * DST], F32))
            CW = E(nc.sbuf_tensor(self.u + "m_cw", [P, DIC * 4], F32))
            CB = E(nc.sbuf_tensor(self.u + "m_cb", [P, DIC], F32))
            DTB2 = E(nc.sbuf_tensor(self.u + "m_dtb2", [P, DIC], F32))
            DSK = E(nc.sbuf_tensor(self.u + "m_dsk", [P, DIC], F32))
            ps_a = E(nc.psum_tensor(self.u + "m_psa", [P, W], F32))
            ps_b = E(nc.psum_tensor(self.u + "m_psb", [P, W], F32))
            ps_x = E(nc.psum_tensor(self.u + "m_psx", [96, W], F32))
            ps_sx = E(nc.psum_tensor(self.u + "m_pssx", [1, W], F32))
            ps_sq = E(nc.psum_tensor(self.u + "m_pssq", [1, W], F32))

            p.dma("gpsimd", XPW[:],
                  bass.AP(d["xproj_w"], 0, [[96, P], [96 * P, DIC], [1, 96]]),
                  sem=gq)
            p.dma("gpsimd", DTW[:],
                  bass.AP(d["dtw"], 0, [[DI, 64], [1, DI]]), sem=gq)
            p.dma("sync", ASB[:], d["a_log"].ap(), sem=ld)
            p.dma("sync", CW[:],
                  bass.AP(d["conv_w"], 0, [[4, P], [4 * P, DIC], [1, 4]]),
                  sem=ld)
            p.dma("sync", CB[:], bass.AP(d["conv_b"], 0, [[1, P], [P, DIC]]),
                  sem=ld, allow_slow_non_contiguous=True)
            p.dma("sync", DTB2[:], bass.AP(d["dtb"], 0, [[1, P], [P, DIC]]),
                  sem=ld, allow_slow_non_contiguous=True)
            p.dma("sync", DSK[:], bass.AP(d["d_skip"], 0, [[1, P], [P, DIC]]),
                  sem=ld, allow_slow_non_contiguous=True)
            consts_ld = ld.n
            consts_gq = gq.n
            p.wait("tensor", gq, consts_gq)
            p.wait("scalar", ld, consts_ld)
            p.emit("scalar", lambda e: e.activation(out=ASB[:], in_=ASB[:],
                                                    func=AF.Exp), inc=act)
            p.wait("vector", act, act.n)
            p.emit("vector", lambda e: e.tensor_scalar_mul(
                out=ASB[:], in0=ASB[:], scalar1=-1.0), inc=vec)
            p.emit("scalar", lambda e: e.mul(out=DTB2[:], in_=DTB2[:],
                                             mul=2.0), inc=act)
            asb_ready = vec.n
            dtb2_ready = act.n

            win_ready = [0, 0]
            win_use = [0, 0]
            wout_ready = [0, 0]
            wout_use = [0, 0]
            op_evac = {0: 0, 1: 0}

            def load_win(o, sl):
                p.wait("gpsimd", ten, win_use[sl])
                p.dma("gpsimd", WINB[sl][:],
                      bass.AP(d["w_inproj"], o * P,
                              [[2 * DI, P], [P * 2 * DI, CH], [1, P]]),
                      sem=gq)
                win_ready[sl] = gq.n

            def load_wout(oc, sl):
                p.wait("gpsimd", ten, wout_use[sl])
                p.dma("gpsimd", WOUTB[sl][:],
                      bass.AP(d["w_outproj"], oc * P,
                              [[CD, P], [P * CD, DIC], [1, P]]),
                      sem=gq)
                wout_ready[sl] = gq.n

            for b in range(NB):
                p.wait("vector", ten, ten.n)
                p.wait("vector", act, act.n)
                p.emit("vector", lambda e: e.memset(STATE[:], 0.0), inc=vec)
                p.emit("vector", lambda e: e.memset(
                    bass.AP(XC, 0, [[DIC * XCW, P], [XCW, DIC], [1, 6]]), 0.0),
                    inc=vec)
                for tc in range(NCH):
                    t0 = b * NSEQ + tc * W
                    prev_vec, prev_ten, prev_act = vec.n, ten.n, act.n
                    p.wait("sync", vec, prev_vec)
                    p.wait("sync", ten, prev_ten)
                    p.wait("sync", act, prev_act)
                    for c in range(CH):
                        p.dma("sync", XT[:, c * W:(c + 1) * W],
                              src.ap()[c * P:(c + 1) * P, t0:t0 + W], sem=ld)
                    p.wait("vector", ld, ld.n)
                    srcs = [XT[:, c * W:(c + 1) * W] for c in range(CH)]
                    self.ln_stats(srcs, W, UT, ps_sx[:], ps_sq[:], ST, MUR, RSR)
                    p.wait("vector", ten, ten.n)
                    for c in range(CH):
                        p.emit("vector", lambda e, c=c: e.tensor_sub(
                            out=UT[:, c * W:(c + 1) * W], in0=srcs[c],
                            in1=MUR[:]), inc=vec)
                        p.wait("vector", vec, vec.n)
                        p.emit("vector", lambda e, c=c: e.tensor_tensor(
                            out=XLB[:, c * W:(c + 1) * W],
                            in0=UT[:, c * W:(c + 1) * W], in1=RSR[:],
                            op=ALU.mult), inc=vec)
                    xlb_ready = vec.n

                    p.wait("tensor", vec, xlb_ready)
                    p.wait("tensor", vec, vec.n)
                    evac_done = {0: act.n, 1: act.n}
                    for o in range(2 * DIC):
                        sl = o % 2
                        if b == 0 and tc == 0 and o == 0:
                            load_win(0, 0)
                        if o + 1 < 2 * DIC:
                            load_win(o + 1, (o + 1) % 2)
                        elif tc + 1 < NCH or b + 1 < NB:
                            load_win(0, (o + 1) % 2)
                        ps = ps_a if sl == 0 else ps_b
                        p.wait("tensor", gq, win_ready[sl])
                        p.wait("tensor", act, evac_done[sl])
                        for c in range(CH):
                            p.emit("tensor", lambda e, c=c, sl=sl, ps=ps:
                                   e.matmul(ps[:],
                                            WINB[sl][:, c * P:(c + 1) * P],
                                            XLB[:, c * W:(c + 1) * W],
                                            start=(c == 0),
                                            stop=(c == CH - 1)),
                                   inc=ten, inc_n=(1 if c == CH - 1 else 0))
                        win_use[sl] = ten.n
                        p.wait("scalar", ten, ten.n)
                        if o < DIC:
                            p.emit("scalar", lambda e, o=o, ps=ps: e.copy(
                                out=XC[:, o * XCW + 6:(o + 1) * XCW],
                                in_=ps[:]), inc=act)
                        else:
                            oz = o - DIC
                            p.emit("scalar", lambda e, oz=oz, ps=ps:
                                   e.activation(
                                       out=ZS[:, oz * W:(oz + 1) * W],
                                       in_=ps[:], func=AF.Silu), inc=act)
                        evac_done[sl] = act.n
                    inproj_done = act.n

                    p.wait("vector", act, inproj_done)
                    csil = {0: 0, 1: 0}
                    for dc in range(DIC):
                        base = dc * XCW
                        cs = dc % 2
                        p.wait("vector", act, csil[cs])
                        p.emit("vector", lambda e, base=base, dc=dc, cs=cs:
                               e.tensor_scalar_mul(
                                   out=CT[:, cs * W:(cs + 1) * W],
                                   in0=XC[:, base:base + W],
                                   scalar1=CW[:, dc * 4:dc * 4 + 1]), inc=vec)
                        for k in range(1, 4):
                            p.wait("vector", vec, vec.n)
                            p.emit("vector", lambda e, base=base, dc=dc, k=k,
                                   cs=cs: e.scalar_tensor_tensor(
                                       out=CT[:, cs * W:(cs + 1) * W],
                                       in0=XC[:, base + k:base + k + W],
                                       scalar=CW[:, dc * 4 + k:dc * 4 + k + 1],
                                       in1=CT[:, cs * W:(cs + 1) * W],
                                       op0=ALU.mult, op1=ALU.add),
                                   inc=vec)
                        p.wait("scalar", vec, vec.n)
                        p.emit("scalar", lambda e, dc=dc, cs=cs: e.activation(
                            out=XBLD[:, dc * W:(dc + 1) * W],
                            in_=CT[:, cs * W:(cs + 1) * W],
                            func=AF.Silu, bias=CB[:, dc:dc + 1]), inc=act)
                        csil[cs] = act.n
                    conv_done = act.n
                    p.wait("vector", act, conv_done)
                    p.emit("vector", lambda e: e.tensor_copy(
                        out=bass.AP(XC, 0, [[DIC * XCW, P], [XCW, DIC], [1, 6]]),
                        in_=bass.AP(XC, W, [[DIC * XCW, P], [XCW, DIC], [1, 6]])),
                        inc=vec)

                    p.wait("tensor", act, conv_done)
                    for dc in range(DIC):
                        p.emit("tensor", lambda e, dc=dc: e.matmul(
                            ps_x[:], XPW[:, dc * 96:(dc + 1) * 96],
                            XBLD[:, dc * W:(dc + 1) * W],
                            start=(dc == 0), stop=(dc == DIC - 1)),
                            inc=ten, inc_n=(1 if dc == DIC - 1 else 0))
                    xproj_done = ten.n
                    p.wait("scalar", ten, xproj_done)
                    p.emit("scalar", lambda e: e.copy(out=DTL[:],
                                                      in_=ps_x[0:64, :]))
                    p.emit("scalar", lambda e: e.copy(out=BCF[64:96, :],
                                                      in_=ps_x[64:96, :]),
                           inc=act)
                    bcf_done = act.n
                    p.wait("gpsimd", act, bcf_done)
                    p.wait("gpsimd", vec, vec.n)
                    p.wait("gpsimd", gq, gq.n)
                    p.dma("gpsimd",
                          bass.AP(d["bcbuf"], tc * W, [[NSEQ, 32], [1, W]]),
                          BCF[64:96, :], sem=gq)
                    bc_written = gq.n
                    p.wait("gpsimd", gq, bc_written)
                    p.dma("gpsimd", BREP[:],
                          bass.AP(d["bcbuf"], tc * W,
                                  [[0, P], [NSEQ, DST], [1, W]]), sem=gq)
                    p.dma("gpsimd", CREP[:],
                          bass.AP(d["bcbuf"], DST * NSEQ + tc * W,
                                  [[0, P], [NSEQ, DST], [1, W]]), sem=gq)
                    brep_ready = gq.n

                    p.wait("tensor", act, act.n)
                    dt_evac = {0: act.n, 1: act.n}
                    for dc in range(DIC):
                        sl = dc % 2
                        ps = ps_a if sl == 0 else ps_b
                        p.wait("tensor", act, dt_evac[sl])
                        p.emit("tensor", lambda e, dc=dc, ps=ps: e.matmul(
                            ps[:], DTW[:, dc * P:(dc + 1) * P], DTL[:],
                            start=True, stop=True), inc=ten)
                        p.wait("scalar", ten, ten.n)
                        p.emit("scalar", lambda e, dc=dc, ps=ps: e.activation(
                            out=SP1[:], in_=ps[:], func=AF.Identity,
                            bias=DTB2[:, dc:dc + 1]), inc=act)
                        dt_evac[sl] = act.n
                        p.wait("vector", act, act.n)
                        p.emit("vector", lambda e: e.tensor_scalar_max(
                            out=SP2[:], in0=SP1[:], scalar1=0.0), inc=vec)
                        p.wait("scalar", vec, vec.n)
                        p.emit("scalar", lambda e: e.activation(
                            out=SP3[:], in_=SP1[:], func=AF.Abs), inc=act)
                        p.wait("scalar", act, act.n)
                        p.emit("scalar", lambda e: e.activation(
                            out=SP3[:], in_=SP3[:], func=AF.Exp, scale=-1.0),
                            inc=act)
                        p.wait("scalar", act, act.n)
                        p.emit("scalar", lambda e: e.activation(
                            out=SP3[:], in_=SP3[:], func=AF.Ln,
                            bias=self.sb["c1"][:]), inc=act)
                        p.wait("vector", act, act.n)
                        p.emit("vector", lambda e, dc=dc: e.tensor_add(
                            out=DELTA[:, dc * W:(dc + 1) * W], in0=SP2[:],
                            in1=SP3[:]))
                        p.emit("vector", lambda e, dc=dc: e.tensor_tensor(
                            out=DU[:, dc * W:(dc + 1) * W],
                            in0=DELTA[:, dc * W:(dc + 1) * W],
                            in1=XBLD[:, dc * W:(dc + 1) * W], op=ALU.mult),
                            inc=vec)
                    delta_done = vec.n

                    p.wait("scalar", vec, delta_done)
                    p.wait("scalar", vec, asb_ready)
                    p.wait("vector", gq, brep_ready)
                    p.wait("vector", ten, ten.n)
                    scans_end = 0
                    for dc in range(DIC):
                        p.wait("scalar", vec, scans_end)
                        for n in range(DST):
                            p.emit("scalar", lambda e, dc=dc, n=n:
                                   e.activation(
                                       out=DA[:, n * W:(n + 1) * W],
                                       in_=DELTA[:, dc * W:(dc + 1) * W],
                                       func=AF.Exp,
                                       scale=ASB[:, dc * DST + n:
                                                 dc * DST + n + 1]),
                                   inc=act, inc_n=(1 if n == DST - 1 else 0))
                        da_done = act.n
                        p.wait("vector", act, da_done)
                        p.emit("vector", lambda e, dc=dc: e.tensor_tensor(
                            out=DB[:].rearrange("p (n w) -> p n w", n=DST),
                            in0=DU[:, dc * W:(dc + 1) * W].unsqueeze(1)
                            .broadcast_to((P, DST, W)),
                            in1=BREP[:].rearrange("p (n w) -> p n w", n=DST),
                            op=ALU.mult), inc=vec)
                        p.wait("vector", vec, vec.n)
                        for n in range(DST):
                            p.emit("vector", lambda e, dc=dc, n=n:
                                   e.tensor_tensor_scan(
                                       out=HS[:, n * W:(n + 1) * W],
                                       data0=DA[:, n * W:(n + 1) * W],
                                       data1=DB[:, n * W:(n + 1) * W],
                                       initial=STATE[:, dc * DST + n:
                                                     dc * DST + n + 1],
                                       op0=ALU.mult, op1=ALU.add), inc=vec)
                        p.wait("vector", vec, vec.n)
                        scans_end = vec.n
                        p.emit("vector", lambda e, dc=dc: e.tensor_copy(
                            out=STATE[:, dc * DST:(dc + 1) * DST],
                            in_=bass.AP(HS, W - 1, [[DST * W, P], [W, DST]])),
                            inc=vec)
                        p.wait("vector", vec, vec.n)
                        p.emit("vector", lambda e: e.tensor_tensor(
                            out=DB[:], in0=HS[:], in1=CREP[:], op=ALU.mult),
                            inc=vec)
                        p.wait("vector", vec, vec.n)
                        p.emit("vector", lambda e: e.tensor_reduce(
                            out=YT[:],
                            in_=bass.AP(DB, 0, [[DST * W, P], [1, W], [W, DST]]),
                            axis=AX.X, op=ALU.add), inc=vec)
                        p.wait("vector", vec, vec.n)
                        p.emit("vector", lambda e, dc=dc:
                               e.scalar_tensor_tensor(
                                   out=YT[:],
                                   in0=XBLD[:, dc * W:(dc + 1) * W],
                                   scalar=DSK[:, dc:dc + 1], in1=YT[:],
                                   op0=ALU.mult, op1=ALU.add), inc=vec)
                        p.wait("vector", vec, vec.n)
                        p.emit("vector", lambda e, dc=dc: e.tensor_tensor(
                            out=Y3[:, dc * W:(dc + 1) * W], in0=YT[:],
                            in1=ZS[:, dc * W:(dc + 1) * W], op=ALU.mult),
                            inc=vec)
                    scan_done = vec.n

                    p.wait("tensor", vec, scan_done)
                    p.wait("tensor", act, act.n)
                    for oc in range(CH):
                        sl = oc % 2
                        if b == 0 and tc == 0 and oc == 0:
                            load_wout(0, 0)
                        if oc + 1 < CH:
                            load_wout(oc + 1, (oc + 1) % 2)
                        elif tc + 1 < NCH or b + 1 < NB:
                            load_wout(0, (oc + 1) % 2)
                        ps = ps_a if sl == 0 else ps_b
                        p.wait("tensor", gq, wout_ready[sl])
                        p.wait("tensor", vec, op_evac[sl])
                        for dc in range(DIC):
                            p.emit("tensor", lambda e, dc=dc, sl=sl, ps=ps:
                                   e.matmul(ps[:],
                                            WOUTB[sl][:, dc * P:(dc + 1) * P],
                                            Y3[:, dc * W:(dc + 1) * W],
                                            start=(dc == 0),
                                            stop=(dc == DIC - 1)),
                                   inc=ten, inc_n=(1 if dc == DIC - 1 else 0))
                        wout_use[sl] = ten.n
                        rsl = oc % 2
                        p.wait("sync", vec, vec.n)
                        p.wait("sync", gq, gq.n)
                        p.dma("sync", RES[:, rsl * W:(rsl + 1) * W],
                              src.ap()[oc * P:(oc + 1) * P, t0:t0 + W],
                              sem=ld)
                        p.wait("vector", ld, ld.n)
                        p.wait("vector", ten, ten.n)
                        p.emit("vector", lambda e, rsl=rsl, ps=ps:
                               e.tensor_add(
                                   out=RES[:, rsl * W:(rsl + 1) * W],
                                   in0=RES[:, rsl * W:(rsl + 1) * W],
                                   in1=ps[:]), inc=vec)
                        op_evac[sl] = vec.n
                        p.wait("gpsimd", vec, vec.n)
                        p.dma("gpsimd",
                              dst.ap()[oc * P:(oc + 1) * P, t0:t0 + W],
                              RES[:, rsl * W:(rsl + 1) * W], sem=gq)
            self.barrier()
            p.flush()

    def finish(self):
        p = self.p
        p.wait("sync", self.gq, self.gq.n)
        p.wait("sync", self.ld, self.ld.n)
        p.wait("sync", self.acc, self.acc.n)
        self.barrier()
        p.flush()
        self.ctx.close()


def build_program(debug=False, do_attn=True, do_ffn1=True, do_mamba=True,
                  do_ffn2=True):
    nc = bass.Bass("TRN2", target_bir_lowering=False)
    b = Builder(nc, debug=debug)
    b.declare_dram()
    b.alloc_common()
    d = b.dram
    cur = d["x_f"]
    if do_attn:
        b.phase_attn(cur, d["r1"])
        cur = d["r1"]
    if do_ffn1:
        nxt = d["r2"] if (do_mamba or do_ffn2) else d["y_f"]
        b.phase_ffn(cur, nxt)
        cur = nxt
    if do_mamba:
        b.phase_mamba(cur, d["r1"])
        cur = d["r1"]
    if do_ffn2:
        b.phase_ffn(cur, d["y_f"])
        cur = d["y_f"]
    if cur is not d["y_f"]:
        b.copy_dram(cur, d["y_f"])
    b.finish()
    return nc


def _copy_dram(self, src, dst):
    nc, p = self.nc, self.p
    self._phase_prefix()
    with ExitStack() as ctx:
        tile = ctx.enter_context(nc.sbuf_tensor(self.u + "cp_t", [P, CH * TM], F32))
        for tm in range(NTM):
            t0 = tm * TM
            p.wait("sync", self.gq, self.gq.n)
            for c in range(CH):
                p.dma("sync", tile[:, c * TM:(c + 1) * TM],
                      src.ap()[c * P:(c + 1) * P, t0:t0 + TM], sem=self.ld)
            p.wait("gpsimd", self.ld, self.ld.n)
            for c in range(CH):
                p.dma("gpsimd", dst.ap()[c * P:(c + 1) * P, t0:t0 + TM],
                      tile[:, c * TM:(c + 1) * TM], sem=self.gq)
        self.barrier()
        p.flush()


Builder.copy_dram = _copy_dram



def _mask3():
    qq = np.arange(P)[:, None]
    jj = np.arange(P)[None, :]
    parts = [((jj - qq + c) >= 31).astype(np.float32) for c in (-P, 0, P)]
    return np.ascontiguousarray(np.concatenate(parts, axis=1))


def _prep_inputs(kw):
    x = np.asarray(kw["L0_attn_a_x"])
    base = {
        "wqkv_t": np.asarray(kw["L0_attn_a_w_qkv"]).T,
        "wproj_t": np.asarray(kw["L0_attn_a_w_proj"]).T,
        "b_proj": kw["L0_attn_a_b_proj"],
        "rpe_q": kw["L0_attn_a_rpe_q_table"],
        "rpe_k": kw["L0_attn_a_rpe_k_table"],
        "rpe_v": kw["L0_attn_a_rpe_v_table"],
        "rpe_v_t": np.asarray(kw["L0_attn_a_rpe_v_table"]).transpose(0, 2, 1),
        "w_inproj": kw["L0_attn_b_in_proj_w"],
        "conv_w": kw["L0_attn_b_conv_w"],
        "conv_b": kw["L0_attn_b_conv_b"],
        "xproj_w": kw["L0_attn_b_x_proj_w"],
        "dtw": kw["L0_attn_b_dt_proj_w"],
        "dtb": kw["L0_attn_b_dt_proj_b"],
        "a_log": np.asarray(kw["L0_attn_b_A_log"]).reshape(DIC, P, DST)
        .transpose(1, 0, 2).reshape(P, DIC * DST),
        "d_skip": kw["L0_attn_b_D_skip"],
        "w_outproj": kw["L0_attn_b_out_proj_w"],
        "ln_w": kw["L0_ffn_ln_w"],
        "ln_b": kw["L0_ffn_ln_b"],
        "router_w": kw["L0_ffn_router_w"],
        "gate_w": kw["L0_ffn_gate_w"],
        "up_w": kw["L0_ffn_up_w"],
        "down_w": kw["L0_ffn_down_w"],
        "sh_gate": kw["L0_ffn_sh_gate"],
        "sh_up": kw["L0_ffn_sh_up"],
        "sh_down": kw["L0_ffn_sh_down"],
        "ident": np.eye(P, dtype=np.float32),
        "mask3": _mask3(),
    }
    base = {k: np.ascontiguousarray(np.asarray(v), dtype=np.float32)
            for k, v in base.items()}
    maps = []
    for core in range(8):
        xf = np.ascontiguousarray(
            np.asarray(x[2 * core:2 * core + 2]).reshape(T, CD).T,
            dtype=np.float32)
        m = dict(base)
        m["x_f"] = xf
        maps.append(m)
    return maps


def kernel(**inputs):
    maps = _prep_inputs(inputs)
    nc = build_program()
    res = run_bass_kernel_spmd(nc, maps, core_ids=list(range(8)))
    out = np.empty((16, NSEQ, CD), np.float32)
    for core in range(8):
        yf = np.asarray(res.results[core]["y_f"])
        out[2 * core:2 * core + 2] = yf.T.reshape(NB, NSEQ, CD)
    return out
